# version 06
# baseline (speedup 1.0000x reference)
import numpy as np
import ml_dtypes
import concourse.bass as bass
import concourse.mybir as mybir
from concourse.bass_utils import run_bass_kernel_spmd

F32 = mybir.dt.float32
BF16 = mybir.dt.bfloat16
AF = mybir.ActivationFunctionType
ALU = mybir.AluOpType
AX = mybir.AxisListType
BF = ml_dtypes.bfloat16

NCORES = 8
T = 8192
D = 2048
TC = T // NCORES
HD = 64
NKV = 8
NQ = 32
DKV = NKV * HD
HGU = 16384
HFF = 8192
RCONST = float(1.5 * 2 ** 23)
LN_EPS = 1e-5
RMS_EPS = float(np.finfo(np.float32).eps)
ENGS = ("sync", "tensor", "vector", "scalar", "gpsimd")
SBUF_TOP = 229344


def _host_consts():
    ident_bf = np.eye(128, dtype=BF)
    ident_f = np.eye(128, dtype=np.float32)
    prope = np.zeros((128, 128), np.float32)
    for k in range(128):
        if k % 64 < 32:
            prope[k, k + 32] = 1.0
        else:
            prope[k, k - 32] = -1.0
    ones_bf = np.ones((128, 128), BF)
    inv_freq = (1.0 / (10000.0 ** (np.arange(0, HD, 2, dtype=np.float32) / HD))).astype(np.float32)
    t = np.arange(T, dtype=np.float32)
    ang = t[:, None] * inv_freq[None, :]
    cos64 = np.concatenate([np.cos(ang), np.cos(ang)], axis=1).T.astype(np.float32)
    sin64 = np.concatenate([np.sin(ang), np.sin(ang)], axis=1).T.astype(np.float32)
    rope_cos = np.ascontiguousarray(np.concatenate([cos64, cos64], axis=0))
    sin_fold = sin64.copy()
    sin_fold[0:32, :] *= -1.0
    rope_sin = np.ascontiguousarray(np.concatenate([sin_fold, sin_fold], axis=0))
    return ident_bf, ident_f, prope.astype(BF), ones_bf, rope_cos, rope_sin


class _Dummy:
    def then_inc(self, *a, **k):
        return self


class _MockEng:
    def __getattr__(self, name):
        def f(*a, **k):
            return _Dummy()
        return f


QS_R = 8
QP_R = 4
SEM_NAMES = (["pe", "act", "dve", "pool",
              "bar_sync", "bar_tensor", "bar_vector", "bar_scalar", "bar_gpsimd"]
             + [f"qs{i}" for i in range(QS_R)] + [f"qp{i}" for i in range(QP_R)])


class Ctx:
    def __init__(self, mock, sems, registry):
        self.mock = mock
        self.sems = sems
        self.reg = registry
        self.counts = {s: 0 for s in SEM_NAMES}
        self.qn = {"qs": 0, "qp": 0}

    def dma(self, E, q, dst, src, name=None):
        R = QS_R if q == "qs" else QP_R
        n = self.qn[q]
        self.qn[q] = n + 1
        snm = f"{q}{n % R}"
        if n >= R and not self.mock:
            E.wait_ge(self.sems[snm], 16 * (n // R))
        inst = E.dma_start(dst, src) if not self.mock else _Dummy()
        self.counts[snm] += 16
        if not self.mock:
            inst.then_inc(self.sems[snm], 16)
        assert self.counts[snm] == 16 * (n // R + 1)
        if name is not None and self.mock:
            assert name not in self.reg, f"duplicate event {name}"
            self.reg[name] = (snm, self.counts[snm])
        return inst

    def inc(self, semname, inst, k=1):
        self.counts[semname] += k
        if not self.mock:
            inst.then_inc(self.sems[semname], k)
        return self.counts[semname]

    def mark(self, semname, inst, name=None, k=1):
        c = self.inc(semname, inst, k)
        if name is not None and self.mock:
            assert name not in self.reg, f"duplicate event {name}"
            self.reg[name] = (semname, c)
        return c

    def alias(self, new, old):
        if self.mock:
            self.reg[new] = self.reg[old]

    def wait(self, E, name):
        if self.mock:
            return
        semname, v = self.reg[name]
        E.wait_ge(self.sems[semname], v)

    def fence(self, E, semname, inst):
        c = self.inc(semname, inst)
        if not self.mock:
            E.wait_ge(self.sems[semname], c)


def build_program(debug=0):
    nc = bass.Bass("TRN2", target_bir_lowering=False)

    dn = {}
    def din(name, shape, dt):
        dn[name] = nc.dram_tensor(name, list(shape), dt, kind="ExternalInput").ap()
    din("x", (T, D), F32)
    din("xo", (TC, D), F32)
    din("wq", (D, D), F32)
    din("wk", (DKV, D), F32)
    din("wv", (DKV, D), F32)
    din("wo", (D, D), F32)
    din("wgu", (HGU, D), F32)
    din("wp", (D, HFF), F32)
    din("bqs", (128, 32), F32)
    din("bkc", (128, 4), F32)
    din("bvc", (128, 4), F32)
    din("boc", (128, 16), F32)
    din("ident_bf", (128, 128), BF16)
    din("ident_f", (128, 128), F32)
    din("prope", (128, 128), BF16)
    din("ones", (128, 128), BF16)
    din("rope_cos_a", (128, T), F32)
    din("rope_sin_a", (128, T), F32)
    din("rope_cos_o", (128, TC), F32)
    din("rope_sin_o", (128, TC), F32)

    y = nc.dram_tensor("y", [TC, D], F32, kind="ExternalOutput").ap()

    k_park = nc.dram_tensor("k_park", [DKV, T], BF16, kind="Internal").ap()
    v_park = nc.dram_tensor("v_park", [T, 1024], BF16, kind="Internal").ap()
    attn_park = nc.dram_tensor("attn_park", [D, TC], F32, kind="Internal").ap()
    den_park = nc.dram_tensor("den_park", [D, TC], F32, kind="Internal").ap()
    ffn_park = nc.dram_tensor("ffn_park", [D, TC], F32, kind="Internal").ap()
    sum_park = nc.dram_tensor("sum_park", [TC, D], F32, kind="Internal").ap()
    dbg_taps = {}
    if debug >= 1:
        for nm, shape, dt in [("t_kpark", [DKV, T], BF16), ("t_vpark", [T, 1024], BF16),
                              ("t_apark", [D, TC], F32), ("t_dpark", [D, TC], F32),
                              ("t_fpark", [D, TC], F32), ("t_spark", [TC, D], F32),
                              ("t_xn", [128, 16 * 1024], BF16), ("t_h", [128, 64 * 1024], BF16),
                              ("t_rstd", [128, 1024], F32),
                              ("t_wq40", [128, 2048], BF16), ("t_wqT40", [128, 2048], BF16),
                              ("t_bt", [128, 1024], F32), ("t_bt2", [128, 1024], F32),
                              ("t_sc", [128, 32], F32), ("t_scb", [128, 32], BF16),
                              ("t_rs", [128, 32], F32), ("t_wh", [128, 1024], BF16)]:
            dbg_taps[nm] = nc.dram_tensor(nm, shape, dt, kind="ExternalOutput").ap()

    def alo(name, shape, dt):
        return nc.alloc_sbuf_tensor(name, list(shape), dt)
    xn = alo("xn", (128, 16 * 1024), BF16)
    idb = alo("idb", (128, 128), BF16)
    prp = alo("prp", (128, 128), BF16)
    onb = alo("onb", (128, 128), BF16)
    bqs = alo("s_bqs", (128, 32), F32)
    bkc = alo("s_bkc", (128, 4), F32)
    bvc = alo("s_bvc", (128, 4), F32)
    boc = alo("s_boc", (128, 16), F32)
    rstd = alo("rstd", (128, 1024), F32)
    smA = alo("smA", (128, 16), F32)
    bcol = alo("bcol", (128, 4), F32)
    nkT = alo("nkT", (128, 4), F32)
    nqT = alo("nqT", (128, 4), F32)

    RB = (nc.sbuf_base + 31) & ~31
    assert RB < 72000, f"persistent allocs too large, RB={RB}"

    def at(name, shape, dt, off):
        nbytes = int(np.prod(shape[1:])) * (2 if dt == BF16 else 4)
        assert RB + off + nbytes <= SBUF_TOP, f"{name} overflows SBUF ({RB+off+nbytes})"
        return nc.alloc_sbuf_tensor_at(name, list(shape), dt, offset=RB + off)

    a_x = [at("a_x0", (128, D), F32, 0), at("a_x1", (128, D), F32, 8192)]
    a_xsq = at("a_xsq", (128, D), F32, 16384)
    a_xtm = at("a_xtm", (128, D), BF16, 24576)
    a_xnf = at("a_xnf", (128, 16 * 1024), BF16, 28672)
    a_wkT = at("a_wkT", (128, 16 * 512), BF16, 61440)
    a_wvT = at("a_wvT", (128, 16 * 512), BF16, 77824)
    a_wst = [at("a_wst0", (128, D), BF16, 94208), at("a_wst1", (128, D), BF16, 98304)]
    a_kst = at("a_kst", (128, 4 * 1024), BF16, 102400)
    a_vst = at("a_vst", (128, 8 * 1024), BF16, 110592)
    a_rc = at("a_rc", (128, 1024), F32, 126976)
    a_rs = at("a_rs", (128, 1024), F32, 131072)
    a_vbf = at("a_vbf", (128, 1024), BF16, 137216)
    a_kc = at("a_kc", (128, 1024), F32, 139264)
    a_kc2 = at("a_kc2", (128, 1024), F32, 143360)
    a_kf = at("a_kf", (128, 1024), F32, 147456)
    a_krot = at("a_krot", (128, 1024), F32, 151552)

    c_kp = [at("c_kp0", (128, T), BF16, 0), at("c_kp1", (128, T), BF16, 16384)]
    c_vp = at("c_vp", (128, 64 * 256), BF16, 32768)
    c_wqs = [at("c_wqs0", (128, D), BF16, 65536), at("c_wqs1", (128, D), BF16, 69632)]
    c_wqT = at("c_wqT", (128, 16 * 512), BF16, 73728)
    c_qp = at("c_qp", (128, 4 * 1024), BF16, 90112)
    c_q2 = at("c_q2", (128, 1024), BF16, 100352)
    c_qc = at("c_qc", (128, 1024), F32, 102400)
    c_qc2 = at("c_qc2", (128, 1024), F32, 106496)
    c_e = [at("c_e0", (128, 1024), BF16, 110592), at("c_e1", (128, 1024), BF16, 112640)]
    c_k2 = at("c_k2", (128, 1024), BF16, 114688)
    c_ab = [at("c_ab0", (128, 1024), F32, 116736), at("c_ab1", (128, 1024), F32, 120832)]
    c_dn = [at("c_dn0", (128, 1024), F32, 124928), at("c_dn1", (128, 1024), F32, 129024)]
    rco = at("rco", (128, TC), F32, 133120)
    rso = at("rso", (128, TC), F32, 137216)
    c_qf = at("c_qf", (128, 1024), F32, 141312)
    c_qrot = at("c_qrot", (128, 1024), F32, 145408)

    b_h = at("b_h", (128, 64 * 1024), BF16, 0)
    b_wg = [at("b_wg0", (128, D), BF16, 131072), at("b_wg1", (128, D), BF16, 135168)]
    b_wu = [at("b_wu0", (128, D), BF16, 139264), at("b_wu1", (128, D), BF16, 143360)]
    b_t = at("b_t", (128, 1024), F32, 147456)
    b_wh = at("b_wh", (128, 1024), BF16, 151552)
    b_wqTg = at("b_wqTg", (128, 2048), BF16, 153600)
    b_wqTu = at("b_wqTu", (128, 2048), BF16, 157696)
    b_sg = at("b_sg", (128, 512), F32, 161792)
    b_h2 = at("b_h2", (128, 1024), BF16, 163840)
    b_sc = at("b_sc", (128, 32), F32, 165888)
    b_scb = at("b_scb", (128, 32), BF16, 166016)
    b_rs = at("b_rs", (128, 32), F32, 166144)
    b_t2 = at("b_t2", (128, 1024), F32, 166272)

    d_wps = [at("d_wps0", (128, 2048), BF16, 131072), at("d_wps1", (128, 2048), BF16, 135168)]
    d_t = at("d_t", (128, 1024), F32, 139264)
    d_wh = at("d_wh", (128, 1024), BF16, 143360)
    d_wpT = at("d_wpT", (128, 2048), BF16, 145408)
    d_sbf = at("d_sbf", (128, 1024), F32, 149504)
    d_sc = at("d_sc", (128, 32), F32, 153600)
    d_scb = at("d_scb", (128, 32), BF16, 153728)
    d_rs = at("d_rs", (128, 32), F32, 153856)
    d_t2 = at("d_t2", (128, 1024), F32, 154112)

    e_attn = at("e_attn", (128, 16 * 1024), BF16, 0)
    e_ab = at("e_ab", (128, 1024), F32, 32768)
    e_dn = at("e_dn", (128, 1024), F32, 36864)
    e_rn = at("e_rn", (128, 1024), F32, 40960)
    e_wos = [at("e_wos0", (128, D), BF16, 45056), at("e_wos1", (128, D), BF16, 49152)]
    e_woT = at("e_woT", (128, 2048), BF16, 53248)
    e_so = at("e_so", (128, 1024), F32, 57344)
    e_ssum = at("e_ssum", (128, 1024), F32, 61440)
    e_ft = [at("e_ft0", (128, 1024), F32, 65536), at("e_ft1", (128, 1024), F32, 69632)]
    e_tm = at("e_tm", (128, 1024), F32, 73728)
    idf = at("idf", (128, 128), F32, 77824)

    f_s = [at("f_s0", (128, D), F32, 0), at("f_s1", (128, D), F32, 8192)]
    f_x = [at("f_x0", (128, D), F32, 16384), at("f_x1", (128, D), F32, 24576)]
    f_o = [at("f_o0", (128, D), F32, 32768), at("f_o1", (128, D), F32, 40960)]

    P = [nc.alloc_psum_tensor(f"P{i}", [128, 1024], F32) for i in range(4)]

    registry = {}

    def P3b():
        return P[3].ap().bitcast(BF16)

    def tp16(E, src_sb, idb_ap):
        p3 = P3b()
        inst = None
        for dc in range(16):
            half, slot = divmod(dc, 8)
            inst = E.transpose(p3[:, half * 1024 + slot * 128: half * 1024 + slot * 128 + 128],
                               src_sb[:, dc * 128:(dc + 1) * 128], idb_ap)
        return inst

    def barrier(ctx, eng, E, ph):
        inst = E.nop() if not ctx.mock else _Dummy()
        ctx.mark("bar_" + eng, inst, f"bar/{ph}/{eng}")
        for o in ENGS:
            if o != eng:
                ctx.wait(E, f"bar/{ph}/{o}")

    def phase_A(ctx, eng, E):
        if eng == "sync":
            for wi in range(40):
                ctx.dma(E, "qs", idb.ap(), dn["ident_bf"], f"A/warm/{wi}")
                ctx.wait(E, f"A/warm/{wi}")
        if eng == "sync":
            for nm, dst in [("ident_bf", idb),
                            ("prope", prp), ("ones", onb), ("bqs", bqs),
                            ("bkc", bkc), ("bvc", bvc), ("boc", boc)]:
                ctx.dma(E, "qs", dst.ap(), dn[nm], f"A/ld/{nm}")
        if eng in ("tensor", "scalar", "vector", "gpsimd"):
            for nm in ("ident_bf", "prope", "ones", "bqs", "bkc", "bvc", "boc"):
                ctx.wait(E, f"A/ld/{nm}")
        for u in range(8):
            wi, r = divmod(u, 4)
            wsrc = "wk" if wi == 0 else "wv"
            wT = a_wkT if wi == 0 else a_wvT
            b = u % 2
            if eng == "gpsimd":
                if u >= 2:
                    ctx.wait(E, f"A/wtp/{u-2}")
                ctx.dma(E, "qp", a_wst[b].ap(), dn[wsrc][r * 128:(r + 1) * 128, :], f"A/wld/{u}")
            if eng == "tensor":
                if u == 0:
                    ctx.wait(E, "A/ld/ident_bf")
                ctx.wait(E, f"A/wld/{u}")
                if u > 0:
                    ctx.wait(E, f"A/wev/{u-1}")
                inst = tp16(E, a_wst[b].ap(), idb.ap())
                ctx.mark("pe", inst, f"A/wtp/{u}")
            if eng == "vector":
                ctx.wait(E, f"A/wtp/{u}")
                p3 = P3b()
                inst = None
                for h in range(2):
                    src = p3[:, h * 1024:(h + 1) * 1024].rearrange("p (s j) -> p s j", j=128)
                    dst = wT.ap().rearrange("p (dc rr j) -> p dc rr j", rr=4, j=128)[
                        :, h * 8:(h + 1) * 8, r, :]
                    inst = E.tensor_copy(dst, src)
                ctx.mark("dve", inst, f"A/wev/{u}")

        seq_base = 0
        for c in range(8):
            xdst = xn if c == 0 else a_xnf
            s_order = [2, 3, 2, 3, 4, 5, 6, 7, 0, 1] if c == 0 else list(range(8))
            for s_i, s in enumerate(s_order):
                n = 8 * c + s
                seq = seq_base + s_i
                b = seq % 2
                if eng == "sync":
                    if seq >= 2:
                        ctx.wait(E, f"A/ap/{seq-2}")
                    ctx.dma(E, "qs", a_x[b].ap(), dn["x"][n * 128:(n + 1) * 128, :], f"A/x/{seq}")
                if eng == "scalar":
                    ctx.wait(E, f"A/x/{seq}")
                    if seq > 0:
                        ctx.wait(E, f"A/st/{seq-1}")
                    inst = E.activation(a_xsq.ap(), a_x[b].ap(), AF.Square)
                    ctx.mark("act", inst, f"A/sq/{seq}")
                    ctx.wait(E, f"A/var/{seq}")
                    inst = E.activation(smA.ap()[:, 4:5], smA.ap()[:, 3:4], AF.Sqrt)
                    ctx.mark("act", inst, f"A/sd/{seq}")
                    ctx.wait(E, f"A/st/{seq}")
                    if seq > 0:
                        ctx.wait(E, f"A/tp/{seq-1}")
                    inst = E.activation(a_xtm.ap(), a_x[b].ap(), AF.Identity,
                                        bias=smA.ap()[:, 5:6], scale=smA.ap()[:, 6:7])
                    ctx.mark("act", inst, f"A/ap/{seq}")
                if eng == "vector":
                    ctx.wait(E, f"A/x/{seq}")
                    E.tensor_reduce(smA.ap()[:, 0:1].unsqueeze(2),
                                    a_x[b].ap().unsqueeze(1), AX.X, ALU.add)
                    ctx.wait(E, f"A/sq/{seq}")
                    inst = E.tensor_reduce(smA.ap()[:, 1:2].unsqueeze(2),
                                           a_xsq.ap().unsqueeze(1), AX.X, ALU.add)
                    ctx.fence(E, "dve", inst)
                    E.tensor_scalar(smA.ap()[:, 2:3], smA.ap()[:, 0:1], 1.0 / D, None, ALU.mult)
                    E.tensor_tensor(smA.ap()[:, 3:4], smA.ap()[:, 2:3], smA.ap()[:, 2:3], ALU.mult)
                    E.tensor_scalar(smA.ap()[:, 3:4], smA.ap()[:, 3:4], LN_EPS, None, ALU.subtract)
                    inst = E.scalar_tensor_tensor(smA.ap()[:, 3:4], smA.ap()[:, 1:2], 1.0 / D,
                                                  smA.ap()[:, 3:4], ALU.mult, ALU.subtract)
                    ctx.mark("dve", inst, f"A/var/{seq}")
                    ctx.wait(E, f"A/sd/{seq}")
                    E.reciprocal(smA.ap()[:, 6:7], smA.ap()[:, 4:5])
                    inst = E.scalar_tensor_tensor(smA.ap()[:, 5:6], smA.ap()[:, 2:3], -1.0,
                                                  smA.ap()[:, 6:7], ALU.mult, ALU.mult)
                    ctx.mark("dve", inst, f"A/st/{seq}")
                if eng == "tensor":
                    ctx.wait(E, f"A/ap/{seq}")
                    if seq == 0:
                        ctx.wait(E, "A/wev/7")
                    else:
                        ctx.wait(E, f"A/xe/{seq-1}")
                    if s_i == 0 and c > 0:
                        ctx.wait(E, f"A/vev/{c-1}/3")
                    inst = tp16(E, a_xtm.ap(), idb.ap())
                    ctx.mark("pe", inst, f"A/tp/{seq}")
                if eng == "vector":
                    ctx.wait(E, f"A/tp/{seq}")
                    p3 = P3b()
                    inst = None
                    for h in range(2):
                        src = p3[:, h * 1024:(h + 1) * 1024].rearrange("p (ss j) -> p ss j", j=128)
                        dst = xdst.ap().rearrange("p (dc t) -> p dc t", t=1024)[
                            :, h * 8:(h + 1) * 8, s * 128:(s + 1) * 128]
                        inst = E.tensor_copy(dst, src)
                    ctx.mark("dve", inst, f"A/xe/{seq}")

            if eng == "sync":
                if c > 0:
                    ctx.wait(E, f"A/krope/{c-1}/3")
                ctx.dma(E, "qs", a_rc.ap(), dn["rope_cos_a"][:, c * 1024:(c + 1) * 1024],
                        f"A/rtc/{c}")
                ctx.dma(E, "qs", a_rs.ap(), dn["rope_sin_a"][:, c * 1024:(c + 1) * 1024],
                        f"A/rt/{c}")

            seq_base += len(s_order)
            for ob in range(4):
                u = 4 * c + ob
                if eng == "tensor":
                    if ob == 0:
                        ctx.wait(E, f"A/xe/{seq_base-1}")
                    if u > 0:
                        ctx.wait(E, f"A/ke/{(u-1)//4}/{(u-1)%4}")
                    inst = None
                    for half in range(2):
                        for dc in range(16):
                            inst = E.matmul(P[0].ap()[:, half * 512:(half + 1) * 512],
                                            a_wkT.ap()[:, dc * 512 + ob * 128: dc * 512 + (ob + 1) * 128],
                                            xdst.ap()[:, dc * 1024 + half * 512: dc * 1024 + (half + 1) * 512],
                                            start=(dc == 0), stop=(dc == 15))
                    ctx.mark("pe", inst, f"A/pk/{c}/{ob}")
                    if u > 0:
                        ctx.wait(E, f"A/vevac/{(u-1)//4}/{(u-1)%4}")
                    for half in range(2):
                        for dc in range(16):
                            inst = E.matmul(P[1].ap()[:, half * 512:(half + 1) * 512],
                                            a_wvT.ap()[:, dc * 512 + ob * 128: dc * 512 + (ob + 1) * 128],
                                            xdst.ap()[:, dc * 1024 + half * 512: dc * 1024 + (half + 1) * 512],
                                            start=(dc == 0), stop=(dc == 15))
                    ctx.mark("pe", inst, f"A/pv/{c}/{ob}")
                if eng == "scalar":
                    ctx.wait(E, f"A/pk/{c}/{ob}")
                    if u > 0:
                        ctx.wait(E, f"A/krope/{(u-1)//4}/{(u-1)%4}")
                        ctx.wait(E, f"A/krot/{(u-1)//4}/{(u-1)%4}")
                    inst = E.activation(a_kf.ap(), P[0].ap(), AF.Identity,
                                        bias=bkc.ap()[:, ob:ob + 1])
                    ctx.mark("act", inst, f"A/ke/{c}/{ob}")
                    ctx.wait(E, f"A/pv/{c}/{ob}")
                    if u > 0:
                        ctx.wait(E, f"A/vtp/{(u-1)//4}/{(u-1)%4}")
                    inst = E.activation(a_vbf.ap(), P[1].ap(), AF.Identity,
                                        bias=bvc.ap()[:, ob:ob + 1])
                    ctx.mark("act", inst, f"A/vevac/{c}/{ob}")
                if eng == "sync":
                    ctx.wait(E, f"A/ke/{c}/{ob}")
                    if u > 0:
                        ctx.wait(E, f"A/krope/{(u-1)//4}/{(u-1)%4}")
                    ctx.dma(E, "qs", a_krot.ap()[0:32, :], a_kf.ap()[32:64, :])
                    ctx.dma(E, "qs", a_krot.ap()[32:64, :], a_kf.ap()[0:32, :])
                    ctx.dma(E, "qs", a_krot.ap()[64:96, :], a_kf.ap()[96:128, :])
                    ctx.dma(E, "qs", a_krot.ap()[96:128, :], a_kf.ap()[64:96, :],
                            f"A/krot/{c}/{ob}")
                if eng == "tensor":
                    ctx.wait(E, f"A/vevac/{c}/{ob}")
                    if u > 0:
                        ctx.wait(E, f"A/vev/{(u-1)//4}/{(u-1)%4}")
                    p3 = P3b()
                    for bb in range(8):
                        inst = E.transpose(p3[:, 1024 + bb * 128: 1024 + (bb + 1) * 128],
                                           a_vbf.ap()[:, bb * 128:(bb + 1) * 128], idb.ap())
                    ctx.mark("pe", inst, f"A/vtp/{c}/{ob}")
                if eng == "vector":
                    ctx.wait(E, f"A/krot/{c}/{ob}")
                    if ob == 0:
                        ctx.wait(E, f"A/rtc/{c}")
                        ctx.wait(E, f"A/rt/{c}")
                        if c > 0:
                            ctx.wait(E, f"A/kpark/{c-1}")
                    E.tensor_tensor(a_kc.ap(), a_kf.ap(), a_rc.ap(), ALU.mult)
                    E.tensor_tensor(a_kc2.ap(), a_krot.ap(), a_rs.ap(), ALU.mult)
                    inst = E.tensor_tensor(a_kst.ap()[:, ob * 1024:(ob + 1) * 1024],
                                           a_kc.ap(), a_kc2.ap(), ALU.add)
                    ctx.mark("dve", inst, f"A/krope/{c}/{ob}")
                    ctx.wait(E, f"A/vtp/{c}/{ob}")
                    if ob == 0:
                        if c > 0:
                            ctx.wait(E, f"A/vpark/{c-1}")
                        E.memset(a_vst.ap().rearrange("p (bb q j) -> p bb q j", q=8, j=128)[
                            :, :, :, 64:128], 1.0)
                    p3 = P3b()
                    src = p3[:, 1024:2048].rearrange("p (bb g j) -> p bb g j", g=2, j=64)
                    dst = a_vst.ap().rearrange("p (bb q j) -> p bb q j", q=8, j=128)[
                        :, :, 2 * ob:2 * ob + 2, 0:64]
                    inst = E.tensor_copy(dst, src)
                    ctx.mark("dve", inst, f"A/vev/{c}/{ob}")
            if eng == "sync":
                ctx.wait(E, f"A/krope/{c}/3")
                dst = k_park[:, c * 1024:(c + 1) * 1024].rearrange("(o p) t -> p o t", p=128)
                ctx.dma(E, "qs", dst, a_kst.ap().rearrange("p (o t) -> p o t", t=1024),
                        f"A/kpark/{c}")
                ctx.wait(E, f"A/vev/{c}/3")
                dst = v_park[c * 1024:(c + 1) * 1024, :].rearrange("(bb p) j -> p bb j", p=128)
                ctx.dma(E, "qs", dst, a_vst.ap().rearrange("p (bb j) -> p bb j", j=1024),
                        f"A/vpark/{c}")
        barrier(ctx, eng, E, "A")

    def phase_C(ctx, eng, E):
        if eng == "sync":
            ctx.dma(E, "qs", rco.ap(), dn["rope_cos_o"], "C/ropeoc")
            ctx.dma(E, "qs", rso.ap(), dn["rope_sin_o"], "C/ropeo")
        for j in range(4):
            kb = j % 2
            lastg = 8 * j - 1
            if eng == "sync":
                if j >= 2:
                    ctx.wait(E, f"C/pdone/{j-2}")
                if j == 0:
                    for cc in range(8):
                        ctx.wait(E, f"A/kpark/{cc}")
                        ctx.wait(E, f"A/vpark/{cc}")
                ctx.dma(E, "qs", c_kp[kb].ap(), k_park[128 * j:128 * (j + 1), :], f"C/kp/{j}")
                if j >= 1:
                    ctx.wait(E, f"C/pdone/{j-1}")
                src = v_park[:, 256 * j:256 * (j + 1)].rearrange("(bb p) i -> p bb i", p=128)
                ctx.dma(E, "qs", c_vp.ap().rearrange("p (bb i) -> p bb i", i=256), src,
                        f"C/vp/{j}")
            for r in range(4):
                wb = (4 * j + r) % 2
                if eng == "gpsimd":
                    idx = 4 * j + r
                    if idx >= 2:
                        ctx.wait(E, f"C/wqtp/{(idx-2)//4}/{(idx-2)%4}")
                    ctx.dma(E, "qp", c_wqs[wb].ap(),
                            dn["wq"][512 * j + 128 * r: 512 * j + 128 * (r + 1), :],
                            f"C/wq/{j}/{r}")
                if eng == "tensor":
                    ctx.wait(E, f"C/wq/{j}/{r}")
                    if 4 * j + r > 0:
                        pj, pr = (j, r - 1) if r > 0 else (j - 1, 3)
                        ctx.wait(E, f"C/wqev/{pj}/{pr}")
                    inst = tp16(E, c_wqs[wb].ap(), idb.ap())
                    ctx.mark("pe", inst, f"C/wqtp/{j}/{r}")
                if eng == "vector":
                    ctx.wait(E, f"C/wqtp/{j}/{r}")
                    if r == 0 and j > 0:
                        ctx.wait(E, f"C/qmm/{j-1}/3")
                    p3 = P3b()
                    inst = None
                    for h in range(2):
                        for g in range(2):
                            m = 2 * r + g
                            qg = 2 * (m % 4) + (m // 4)
                            src = p3[:, h * 1024:(h + 1) * 1024].rearrange(
                                "p (ss gg i) -> p ss gg i", gg=2, i=64)[:, :, g, :]
                            dst = c_wqT.ap().rearrange("p (dc qq i) -> p dc qq i", qq=8, i=64)[
                                :, h * 8:(h + 1) * 8, qg, :]
                            inst = E.tensor_copy(dst, src)
                    ctx.mark("dve", inst, f"C/wqev/{j}/{r}")
            for cc in range(8):
                if eng == "scalar":
                    if cc == 0:
                        ctx.wait(E, f"C/kp/{j}")
                    else:
                        ctx.wait(E, f"C/k2mm/{j}/{cc-1}")
                    inst = E.activation(c_k2.ap(), c_kp[kb].ap()[:, cc * 1024:(cc + 1) * 1024],
                                        AF.Square)
                    ctx.mark("act", inst, f"C/k2/{j}/{cc}")
                if eng == "tensor":
                    ctx.wait(E, f"C/k2/{j}/{cc}")
                    if cc == 0:
                        ctx.wait(E, f"C/wqev/{j}/3")
                    else:
                        ctx.wait(E, f"C/k2red/{j}/{cc-1}")
                    inst = None
                    for half in range(2):
                        inst = E.matmul(P[3].ap()[:, half * 512:(half + 1) * 512], onb.ap(),
                                        c_k2.ap()[:, half * 512:(half + 1) * 512],
                                        start=True, stop=True)
                    ctx.mark("pe", inst, f"C/k2mm/{j}/{cc}")
                if eng == "vector":
                    ctx.wait(E, f"C/k2mm/{j}/{cc}")
                    if cc == 0 and j > 0:
                        ctx.wait(E, f"C/nk/{j-1}")
                    inst = E.tensor_reduce(nkT.ap()[:, 2:3].unsqueeze(2), P[3].ap().unsqueeze(1),
                                           AX.X, ALU.max)
                    ctx.fence(E, "dve", inst)
                    if cc == 0:
                        inst = E.tensor_copy(nkT.ap()[:, 3:4], nkT.ap()[:, 2:3])
                    else:
                        inst = E.tensor_tensor(nkT.ap()[:, 3:4], nkT.ap()[:, 3:4],
                                               nkT.ap()[:, 2:3], ALU.max)
                    ctx.mark("dve", inst, f"C/k2red/{j}/{cc}")
            if eng == "scalar":
                ctx.wait(E, f"C/k2red/{j}/7")
                inst = E.activation(nkT.ap()[:, 0:1], nkT.ap()[:, 3:4], AF.Sqrt)
                ctx.mark("act", inst, f"C/nk/{j}")

            for i in range(4):
                if eng == "tensor":
                    if i == 0:
                        ctx.wait(E, f"C/wqev/{j}/3")
                        ctx.wait(E, f"C/k2red/{j}/7")
                    else:
                        ctx.wait(E, f"C/qev/{j}/{i-1}")
                    inst = None
                    for half in range(2):
                        for dc in range(16):
                            inst = E.matmul(P[3].ap()[:, half * 512:(half + 1) * 512],
                                            c_wqT.ap()[:, dc * 512 + i * 128: dc * 512 + i * 128 + 128],
                                            xn.ap()[:, dc * 1024 + half * 512: dc * 1024 + (half + 1) * 512],
                                            start=(dc == 0), stop=(dc == 15))
                    ctx.mark("pe", inst, f"C/qmm/{j}/{i}")
                if eng == "scalar":
                    ctx.wait(E, f"C/qmm/{j}/{i}")
                    if 4 * j + i > 0:
                        pj, pi = (j, i - 1) if i > 0 else (j - 1, 3)
                        ctx.wait(E, f"C/qrope/{pj}/{pi}")
                        ctx.wait(E, f"C/qrot/{pj}/{pi}")
                    inst = E.activation(c_qf.ap(), P[3].ap(), AF.Identity,
                                        bias=bqs.ap()[:, 4 * j + i: 4 * j + i + 1])
                    ctx.mark("act", inst, f"C/qev/{j}/{i}")
                if eng == "sync":
                    ctx.wait(E, f"C/qev/{j}/{i}")
                    if 4 * j + i > 0:
                        pj, pi = (j, i - 1) if i > 0 else (j - 1, 3)
                        ctx.wait(E, f"C/qrope/{pj}/{pi}")
                    ctx.dma(E, "qs", c_qrot.ap()[0:32, :], c_qf.ap()[32:64, :])
                    ctx.dma(E, "qs", c_qrot.ap()[32:64, :], c_qf.ap()[0:32, :])
                    ctx.dma(E, "qs", c_qrot.ap()[64:96, :], c_qf.ap()[96:128, :])
                    ctx.dma(E, "qs", c_qrot.ap()[96:128, :], c_qf.ap()[64:96, :],
                            f"C/qrot/{j}/{i}")
                if eng == "vector":
                    ctx.wait(E, f"C/qrot/{j}/{i}")
                    if i == 0 and j == 0:
                        ctx.wait(E, "C/ropeoc")
                        ctx.wait(E, "C/ropeo")
                    if i == 0 and j > 0:
                        ctx.wait(E, f"C/sc/{lastg}/63")
                    E.tensor_tensor(c_qc.ap(), c_qf.ap(), rco.ap(), ALU.mult)
                    E.tensor_tensor(c_qc2.ap(), c_qrot.ap(), rso.ap(), ALU.mult)
                    inst = E.tensor_tensor(c_qp.ap()[:, i * 1024:(i + 1) * 1024],
                                           c_qc.ap(), c_qc2.ap(), ALU.add)
                    ctx.mark("dve", inst, f"C/qrope/{j}/{i}")
                if eng == "scalar":
                    ctx.wait(E, f"C/qrope/{j}/{i}")
                    if i > 0:
                        ctx.wait(E, f"C/q2mm/{j}/{i-1}")
                    inst = E.activation(c_q2.ap(), c_qp.ap()[:, i * 1024:(i + 1) * 1024], AF.Square)
                    ctx.mark("act", inst, f"C/q2/{j}/{i}")
                if eng == "tensor":
                    ctx.wait(E, f"C/q2/{j}/{i}")
                    if i > 0:
                        ctx.wait(E, f"C/q2red/{j}/{i-1}")
                    elif j > 0:
                        ctx.wait(E, f"C/expall/{lastg}")
                    inst = None
                    for half in range(2):
                        inst = E.matmul(P[1].ap()[:, half * 512:(half + 1) * 512], onb.ap(),
                                        c_q2.ap()[:, half * 512:(half + 1) * 512],
                                        start=True, stop=True)
                    ctx.mark("pe", inst, f"C/q2mm/{j}/{i}")
                if eng == "vector":
                    ctx.wait(E, f"C/q2mm/{j}/{i}")
                    if i > 0:
                        ctx.wait(E, f"C/nq/{j}/{i-1}")
                    inst = E.tensor_reduce(nqT.ap()[:, 0:1].unsqueeze(2), P[1].ap().unsqueeze(1),
                                           AX.X, ALU.max)
                    ctx.mark("dve", inst, f"C/q2red/{j}/{i}")
                if eng == "scalar":
                    ctx.wait(E, f"C/q2red/{j}/{i}")
                    inst = E.activation(nqT.ap()[:, 1:2], nqT.ap()[:, 0:1], AF.Sqrt)
                    ctx.mark("act", inst, f"C/nq/{j}/{i}")
                if eng == "vector":
                    ctx.wait(E, f"C/nq/{j}/{i}")
                    ctx.wait(E, f"C/nk/{j}")
                    if i == 0 and j > 0:
                        ctx.wait(E, f"C/expall/{lastg}")
                    inst = E.scalar_tensor_tensor(bcol.ap()[:, i:i + 1], nqT.ap()[:, 1:2],
                                                  -0.125, nkT.ap()[:, 0:1], ALU.mult, ALU.mult)
                    ctx.mark("dve", inst, f"C/bh/{j}/{i}")

            for m in range(8):
                i, p = divmod(m, 2)
                gh = 8 * j + m
                hh = 8 * j + 4 * p + i
                if eng == "tensor":
                    if m == 0:
                        ctx.wait(E, f"C/vp/{j}")
                        ctx.wait(E, f"C/bh/{j}/3")
                    inst = None
                    for tkb in range(64):
                        sb_i = tkb % 2
                        if tkb >= 2:
                            ctx.wait(E, f"C/exp/{gh}/{tkb-2}")
                        elif gh > 0:
                            ctx.wait(E, f"C/expall/{gh-1}")
                        for half in range(2):
                            inst = E.matmul(
                                P[sb_i].ap()[:, half * 512:(half + 1) * 512],
                                c_kp[kb].ap()[64 * p:64 * (p + 1), tkb * 128:(tkb + 1) * 128],
                                c_qp.ap()[64 * p:64 * (p + 1),
                                          i * 1024 + half * 512: i * 1024 + half * 512 + 512],
                                start=True, stop=True)
                        ctx.mark("pe", inst, f"C/sc/{gh}/{tkb}")
                        ctx.wait(E, f"C/exp/{gh}/{tkb}")
                        if tkb == 0 and gh > 0:
                            ctx.wait(E, f"C/aev/{gh-1}")
                            ctx.wait(E, f"C/dev/{gh-1}")
                        for half in range(2):
                            inst = E.matmul(
                                P[2].ap()[:, half * 512:(half + 1) * 512],
                                c_vp.ap()[:, tkb * 256 + p * 128: tkb * 256 + (p + 1) * 128],
                                c_e[tkb % 2].ap()[:, half * 512:(half + 1) * 512],
                                start=(tkb == 0), stop=(tkb == 63))
                        cnum = ctx.mark("pe", inst, f"C/av/{gh}/{tkb}")
                        if tkb == 63 and ctx.mock:
                            registry[f"C/avall/{gh}"] = ("pe", cnum)
                if eng == "scalar":
                    for tkb in range(64):
                        ctx.wait(E, f"C/sc/{gh}/{tkb}")
                        if tkb >= 2:
                            ctx.wait(E, f"C/av/{gh}/{tkb-2}")
                        elif gh > 0:
                            ctx.wait(E, f"C/avall/{gh-1}")
                        inst = E.activation(c_e[tkb % 2].ap(), P[tkb % 2].ap(), AF.Exp,
                                            bias=bcol.ap()[:, i:i + 1], scale=0.125)
                        cnum = ctx.mark("act", inst, f"C/exp/{gh}/{tkb}")
                        if tkb == 63 and ctx.mock:
                            registry[f"C/expall/{gh}"] = ("act", cnum)
                    ctx.wait(E, f"C/av/{gh}/63")
                    if gh >= 2:
                        ctx.wait(E, f"C/apark/{gh-2}")
                    inst = E.activation(c_ab[gh % 2].ap()[0:64, :], P[2].ap()[0:64, :],
                                        AF.Identity)
                    ctx.mark("act", inst, f"C/aev/{gh}")
                if eng == "vector":
                    ctx.wait(E, f"C/av/{gh}/63")
                    if gh >= 2:
                        ctx.wait(E, f"C/dpark/{gh-2}")
                    inst = E.tensor_copy(c_dn[gh % 2].ap()[64:128, :], P[2].ap()[64:128, :])
                    ctx.mark("dve", inst, f"C/dev/{gh}")
                if eng == "sync":
                    ctx.wait(E, f"C/aev/{gh}")
                    ctx.dma(E, "qs", attn_park[64 * hh:64 * (hh + 1), :],
                            c_ab[gh % 2].ap()[0:64, :], f"C/apark/{gh}")
                    ctx.wait(E, f"C/dev/{gh}")
                    ctx.dma(E, "qs", den_park[64 * hh:64 * (hh + 1), :],
                            c_dn[gh % 2].ap()[64:128, :], f"C/dpark/{gh}")
            if eng == "sync" and ctx.mock:
                registry[f"C/pdone/{j}"] = registry[f"C/dpark/{8*j+7}"]
        barrier(ctx, eng, E, "C")

    def quant_tile(ctx, eng, E, tag, wtile, W, sc, scb, rs, tbuf, tbuf2, whbuf):
        ng = W // 64
        nh = W // 1024
        if eng == "vector":
            inst = E.tensor_reduce(sc.ap()[:, 0:ng].unsqueeze(2),
                                   wtile.rearrange("p (g e) -> p g e", e=64),
                                   AX.X, ALU.add, apply_absolute_value=True)
            ctx.fence(E, "dve", inst)
            inst = E.tensor_scalar(scb.ap()[:, 0:ng], sc.ap()[:, 0:ng], 1.0 / 64, 1e-8,
                                   ALU.mult, ALU.max)
            ctx.mark("dve", inst, f"{tag}/scb")
        if eng == "scalar":
            ctx.wait(E, f"{tag}/scb")
            inst = E.activation(sc.ap()[:, 0:ng], scb.ap()[:, 0:ng], AF.Identity)
            ctx.mark("act", inst, f"{tag}/scf")
        if eng == "vector":
            ctx.wait(E, f"{tag}/scf")
            inst = E.reciprocal(rs.ap()[:, 0:ng], sc.ap()[:, 0:ng])
            ctx.mark("dve", inst, f"{tag}/rs")
        for h in range(nh):
            wsl = wtile[:, h * 1024:(h + 1) * 1024].rearrange("p (g e) -> p g e", e=64)
            rsb = rs.ap()[:, h * 16:(h + 1) * 16].unsqueeze(2).broadcast_to([128, 16, 64])
            scbb = sc.ap()[:, h * 16:(h + 1) * 16].unsqueeze(2).broadcast_to([128, 16, 64])
            if eng == "vector":
                E.tensor_tensor(tbuf.ap().rearrange("p (g e) -> p g e", e=64),
                                wsl, rsb, ALU.mult)
                E.tensor_scalar_add(tbuf2.ap(), tbuf.ap(), RCONST)
                E.tensor_scalar_sub(tbuf.ap(), tbuf2.ap(), RCONST)
                E.tensor_scalar(tbuf2.ap(), tbuf.ap(), -1.0, 1.0, ALU.max, ALU.min)
                E.tensor_tensor(whbuf.ap().rearrange("p (g e) -> p g e", e=64),
                                tbuf2.ap().rearrange("p (g e) -> p g e", e=64), scbb, ALU.mult)
                E.tensor_tensor(whbuf.ap(), whbuf.ap(), wtile[:, h * 1024:(h + 1) * 1024],
                                ALU.subtract)
                E.tensor_tensor(wtile[:, h * 1024:(h + 1) * 1024],
                                wtile[:, h * 1024:(h + 1) * 1024], whbuf.ap(), ALU.add)
                inst = E.tensor_scalar_add(rs.ap()[:, 0:1], rs.ap()[:, 0:1], 0.0)
                ctx.mark("dve", inst, f"{tag}/w/{h}")
        if eng == "vector":
            ctx.alias(f"{tag}/done", f"{tag}/w/{nh-1}")

    def phase_B(ctx, eng, E):
        for k in range(64):
            wb = k % 2
            if eng == "gpsimd":
                if k >= 2:
                    ctx.wait(E, f"B/tpg/{k-2}")
                    ctx.wait(E, f"B/tpu/{k-2}")
                ctx.dma(E, "qp", b_wg[wb].ap(), dn["wgu"][128 * k:128 * (k + 1), :], f"B/lg/{k}")
                ctx.dma(E, "qp", b_wu[wb].ap(),
                        dn["wgu"][HFF + 128 * k: HFF + 128 * (k + 1), :], f"B/lu/{k}")
            if eng == "vector":
                ctx.wait(E, f"B/lg/{k}")
            if eng in ("vector", "gpsimd", "scalar"):
                quant_tile(ctx, eng, E, f"B/qg/{k}", b_wg[wb].ap(), D, b_sc, b_scb, b_rs, b_t, b_t2, b_wh)
            if debug >= 2 and k == 40 and eng == "vector":
                for nm in ("t_bt", "t_bt2", "t_sc", "t_scb", "t_rs", "t_wh"):
                    ctx.wait(E, f"DBG/{nm}")
            if eng == "vector":
                ctx.wait(E, f"B/lu/{k}")
            if eng in ("vector", "gpsimd", "scalar"):
                quant_tile(ctx, eng, E, f"B/qu/{k}", b_wu[wb].ap(), D, b_sc, b_scb, b_rs, b_t, b_t2, b_wh)
            if debug >= 2 and k == 40 and eng == "sync":
                ctx.wait(E, "B/qg/40/done")
                ctx.dma(E, "qs", dbg_taps["t_wq40"], b_wg[0].ap(), "DBG/wq40")
                for nm, src in [("t_bt", b_t), ("t_bt2", b_t2), ("t_sc", b_sc),
                                ("t_scb", b_scb), ("t_rs", b_rs), ("t_wh", b_wh)]:
                    ctx.dma(E, "qs", dbg_taps[nm], src.ap(), f"DBG/{nm}")
            if debug >= 2 and k == 40 and eng == "vector":
                ctx.wait(E, "DBG/wq40")
                inst = E.nop()
                ctx.mark("dve", inst, "DBG/wq40ok")
            if debug >= 2 and k == 42 and eng == "gpsimd":
                ctx.wait(E, "DBG/wq40ok")
            if eng == "tensor":
                ctx.wait(E, f"B/qg/{k}/done")
                if debug >= 2 and k == 40:
                    ctx.wait(E, "DBG/wq40ok")
                if k > 0:
                    ctx.wait(E, f"B/evu/{k-1}")
                inst = tp16(E, b_wg[wb].ap(), idb.ap())
                ctx.mark("pe", inst, f"B/tpg/{k}")
            if eng == "vector":
                ctx.wait(E, f"B/tpg/{k}")
                if k > 0:
                    ctx.wait(E, f"B/mmg/{k-1}")
                p3 = P3b()
                inst = None
                for h in range(2):
                    inst = E.tensor_copy(b_wqTg.ap()[:, h * 1024:(h + 1) * 1024],
                                         p3[:, h * 1024:(h + 1) * 1024])
                ctx.mark("dve", inst, f"B/evg/{k}")
            if debug >= 2 and k == 40 and eng == "sync":
                ctx.wait(E, "B/evg/40")
                ctx.dma(E, "qs", dbg_taps["t_wqT40"], b_wqTg.ap(), "DBG/wqT40")
            if debug >= 2 and k == 41 and eng == "tensor":
                ctx.wait(E, "DBG/wqT40")
            if eng == "tensor":
                ctx.wait(E, f"B/evg/{k}")
                ctx.wait(E, f"B/qu/{k}/done")
                inst = tp16(E, b_wu[wb].ap(), idb.ap())
                ctx.mark("pe", inst, f"B/tpu/{k}")
            if eng == "vector":
                ctx.wait(E, f"B/tpu/{k}")
                if k > 0:
                    ctx.wait(E, f"B/mmu/{k-1}")
                p3 = P3b()
                inst = None
                for h in range(2):
                    inst = E.tensor_copy(b_wqTu.ap()[:, h * 1024:(h + 1) * 1024],
                                         p3[:, h * 1024:(h + 1) * 1024])
                ctx.mark("dve", inst, f"B/evu/{k}")
            if eng == "tensor":
                ctx.wait(E, f"B/evg/{k}")
                if k > 0:
                    ctx.wait(E, f"B/sg/{k-1}/1")
                inst = None
                for half in range(2):
                    for dc in range(16):
                        inst = E.matmul(P[0].ap()[:, half * 512:(half + 1) * 512],
                                        b_wqTg.ap()[:, dc * 128:(dc + 1) * 128],
                                        xn.ap()[:, dc * 1024 + half * 512: dc * 1024 + (half + 1) * 512],
                                        start=(dc == 0), stop=(dc == 15))
                ctx.mark("pe", inst, f"B/mmg/{k}")
                ctx.wait(E, f"B/evu/{k}")
                if k > 0:
                    ctx.wait(E, f"B/hm/{k-1}/1")
                for half in range(2):
                    for dc in range(16):
                        inst = E.matmul(P[1].ap()[:, half * 512:(half + 1) * 512],
                                        b_wqTu.ap()[:, dc * 128:(dc + 1) * 128],
                                        xn.ap()[:, dc * 1024 + half * 512: dc * 1024 + (half + 1) * 512],
                                        start=(dc == 0), stop=(dc == 15))
                ctx.mark("pe", inst, f"B/mmu/{k}")
            if eng == "scalar":
                for half in range(2):
                    ctx.wait(E, f"B/mmg/{k}")
                    if half == 0:
                        if k > 0:
                            ctx.wait(E, f"B/hm/{k-1}/1")
                    else:
                        ctx.wait(E, f"B/hm/{k}/0")
                    inst = E.activation(b_sg.ap(), P[0].ap()[:, half * 512:(half + 1) * 512],
                                        AF.Silu)
                    ctx.mark("act", inst, f"B/sg/{k}/{half}")
            if eng == "vector":
                for half in range(2):
                    ctx.wait(E, f"B/sg/{k}/{half}")
                    ctx.wait(E, f"B/mmu/{k}")
                    inst = E.tensor_tensor(
                        b_h.ap()[:, k * 1024 + half * 512: k * 1024 + (half + 1) * 512],
                        P[1].ap()[:, half * 512:(half + 1) * 512], b_sg.ap(), ALU.mult)
                    ctx.mark("dve", inst, f"B/hm/{k}/{half}")
                if k > 0:
                    ctx.wait(E, f"B/rmm/{k-1}")
                inst = E.tensor_tensor(b_h2.ap(), b_h.ap()[:, k * 1024:(k + 1) * 1024],
                                       b_h.ap()[:, k * 1024:(k + 1) * 1024], ALU.mult)
                ctx.mark("dve", inst, f"B/h2/{k}")
            if eng == "tensor":
                ctx.wait(E, f"B/h2/{k}")
                inst = None
                for half in range(2):
                    inst = E.matmul(P[2].ap()[:, half * 512:(half + 1) * 512], onb.ap(),
                                    b_h2.ap()[:, half * 512:(half + 1) * 512],
                                    start=(k == 0), stop=(k == 63))
                ctx.mark("pe", inst, f"B/rmm/{k}")
        if eng == "vector":
            ctx.wait(E, "B/rmm/63")
            inst = E.tensor_scalar(rstd.ap(), P[2].ap(), 1.0 / HFF, RMS_EPS, ALU.mult, ALU.add)
            ctx.mark("dve", inst, "B/rv0")
        if eng == "scalar":
            ctx.wait(E, "B/rv0")
            inst = E.activation(rstd.ap(), rstd.ap(), AF.Sqrt)
            ctx.mark("act", inst, "B/rv1")
        if eng == "vector":
            ctx.wait(E, "B/rv1")
            inst = E.reciprocal(rstd.ap(), rstd.ap())
            ctx.mark("dve", inst, "B/rstd")
        barrier(ctx, eng, E, "B")

    def phase_D1(ctx, eng, E):
        for db in range(16):
            for hc4 in range(4):
                sl = db * 4 + hc4
                wb = sl % 2
                if eng == "gpsimd":
                    if sl >= 2:
                        ctx.wait(E, f"D1/tp/{sl-2}")
                    ctx.dma(E, "qp", d_wps[wb].ap(),
                            dn["wp"][128 * db:128 * (db + 1), 2048 * hc4:2048 * (hc4 + 1)],
                            f"D1/l/{sl}")
                if eng == "vector":
                    ctx.wait(E, f"D1/l/{sl}")
                if eng in ("vector", "gpsimd", "scalar"):
                    quant_tile(ctx, eng, E, f"D1/q/{sl}", d_wps[wb].ap(), D,
                               d_sc, d_scb, d_rs, d_t, d_t2, d_wh)
                if eng == "tensor":
                    ctx.wait(E, f"D1/q/{sl}/done")
                    if sl > 0:
                        ctx.wait(E, f"D1/ev/{sl-1}")
                    inst = tp16(E, d_wps[wb].ap(), idb.ap())
                    ctx.mark("pe", inst, f"D1/tp/{sl}")
                if eng == "vector":
                    ctx.wait(E, f"D1/tp/{sl}")
                    if sl > 0:
                        ctx.wait(E, f"D1/mm/{sl-1}")
                    p3 = P3b()
                    inst = None
                    for h in range(2):
                        inst = E.tensor_copy(d_wpT.ap()[:, h * 1024:(h + 1) * 1024],
                                             p3[:, h * 1024:(h + 1) * 1024])
                    ctx.mark("dve", inst, f"D1/ev/{sl}")
                if eng == "tensor":
                    ctx.wait(E, f"D1/ev/{sl}")
                    if hc4 == 0 and db > 0:
                        ctx.wait(E, f"D1/sb/{db-1}")
                    inst = None
                    for half in range(2):
                        for hb in range(16):
                            inst = E.matmul(
                                P[0].ap()[:, half * 512:(half + 1) * 512],
                                d_wpT.ap()[:, hb * 128:(hb + 1) * 128],
                                b_h.ap()[:, (hc4 * 16 + hb) * 1024 + half * 512:
                                         (hc4 * 16 + hb) * 1024 + (half + 1) * 512],
                                start=(hc4 == 0 and hb == 0), stop=(hc4 == 3 and hb == 15))
                    ctx.mark("pe", inst, f"D1/mm/{sl}")
            if eng == "vector":
                ctx.wait(E, f"D1/mm/{db*4+3}")
                if db > 0:
                    ctx.wait(E, f"D1/fpark/{db-1}")
                inst = E.tensor_tensor(d_sbf.ap(), P[0].ap(), rstd.ap(), ALU.mult)
                ctx.mark("dve", inst, f"D1/sb/{db}")
            if eng == "sync":
                ctx.wait(E, f"D1/sb/{db}")
                ctx.dma(E, "qs", ffn_park[128 * db:128 * (db + 1), :], d_sbf.ap(),
                        f"D1/fpark/{db}")
        barrier(ctx, eng, E, "D1")

    def phase_D2(ctx, eng, E):
        if eng == "sync":
            ctx.dma(E, "qs", idf.ap(), dn["ident_f"], "D2/idf")
        for dc in range(16):
            if eng == "sync":
                if dc > 0:
                    ctx.wait(E, f"D2/nrm/{dc-1}")
                for hh in (2 * dc, 2 * dc + 1):
                    jj, rr = divmod(hh, 8)
                    ghx = 8 * jj + 2 * (rr % 4) + (rr // 4)
                    ctx.wait(E, f"C/apark/{ghx}")
                    ctx.wait(E, f"C/dpark/{ghx}")
                ctx.dma(E, "qs", e_ab.ap(), attn_park[128 * dc:128 * (dc + 1), :],
                        f"D2/lab/{dc}")
                ctx.dma(E, "qs", e_dn.ap(), den_park[128 * dc:128 * (dc + 1), :],
                        f"D2/la/{dc}")
            if eng == "vector":
                ctx.wait(E, f"D2/lab/{dc}")
                ctx.wait(E, f"D2/la/{dc}")
                E.reciprocal(e_rn.ap(), e_dn.ap())
                inst = E.tensor_tensor(e_attn.ap()[:, dc * 1024:(dc + 1) * 1024],
                                       e_ab.ap(), e_rn.ap(), ALU.mult)
                ctx.mark("dve", inst, f"D2/nrm/{dc}")
        for db in range(16):
            wb = db % 2
            if eng == "gpsimd":
                if db >= 2:
                    ctx.wait(E, f"D2/tp/{db-2}")
                ctx.dma(E, "qp", e_wos[wb].ap(), dn["wo"][128 * db:128 * (db + 1), :],
                        f"D2/lw/{db}")
            if eng == "tensor":
                ctx.wait(E, f"D2/lw/{db}")
                if db > 0:
                    ctx.wait(E, f"D2/ev/{db-1}")
                inst = tp16(E, e_wos[wb].ap(), idb.ap())
                ctx.mark("pe", inst, f"D2/tp/{db}")
            if eng == "vector":
                ctx.wait(E, f"D2/tp/{db}")
                if db > 0:
                    ctx.wait(E, f"D2/mm/{db-1}")
                p3 = P3b()
                inst = None
                for h in range(2):
                    inst = E.tensor_copy(e_woT.ap()[:, h * 1024:(h + 1) * 1024],
                                         p3[:, h * 1024:(h + 1) * 1024])
                ctx.mark("dve", inst, f"D2/ev/{db}")
            if eng == "tensor":
                ctx.wait(E, f"D2/ev/{db}")
                if db == 0:
                    ctx.wait(E, "D2/nrm/15")
                else:
                    ctx.wait(E, f"D2/so/{db-1}")
                inst = None
                for half in range(2):
                    for dc in range(16):
                        inst = E.matmul(P[0].ap()[:, half * 512:(half + 1) * 512],
                                        e_woT.ap()[:, dc * 128:(dc + 1) * 128],
                                        e_attn.ap()[:, dc * 1024 + half * 512: dc * 1024 + (half + 1) * 512],
                                        start=(dc == 0), stop=(dc == 15))
                ctx.mark("pe", inst, f"D2/mm/{db}")
            if eng == "sync":
                fb = db % 2
                if db >= 2:
                    ctx.wait(E, f"D2/ss/{db-2}")
                ctx.wait(E, f"D1/fpark/{db}")
                ctx.dma(E, "qs", e_ft[fb].ap(), ffn_park[128 * db:128 * (db + 1), :],
                        f"D2/lf/{db}")
            if eng == "vector":
                ctx.wait(E, f"D2/mm/{db}")
                inst = E.tensor_scalar(e_so.ap(), P[0].ap(), boc.ap()[:, db:db + 1], None, ALU.add)
                ctx.mark("dve", inst, f"D2/so/{db}")
                ctx.wait(E, f"D2/lf/{db}")
                if db > 0:
                    ctx.wait(E, f"D2/tst/{db-1}")
                inst = E.tensor_tensor(e_ssum.ap(), e_so.ap(), e_ft[db % 2].ap(), ALU.add)
                ctx.mark("dve", inst, f"D2/ss/{db}")
            if eng == "tensor":
                ctx.wait(E, f"D2/ss/{db}")
                if db == 0:
                    ctx.wait(E, "D2/idf")
                else:
                    ctx.wait(E, f"D2/tme/{db-1}")
                inst = None
                for q in range(8):
                    inst = E.transpose(P[1].ap()[:, q * 128:(q + 1) * 128],
                                       e_ssum.ap()[:, q * 128:(q + 1) * 128], idf.ap())
                ctx.mark("pe", inst, f"D2/tst/{db}")
            if eng == "vector":
                ctx.wait(E, f"D2/tst/{db}")
                if db > 0:
                    ctx.wait(E, f"D2/spark/{db-1}")
                inst = E.tensor_copy(e_tm.ap(), P[1].ap())
                ctx.mark("dve", inst, f"D2/tme/{db}")
            if eng == "sync":
                ctx.wait(E, f"D2/tme/{db}")
                dst = sum_park[:, 128 * db:128 * (db + 1)].rearrange("(q p) d -> p q d", p=128)
                ctx.dma(E, "qs", dst, e_tm.ap().rearrange("p (q d) -> p q d", d=128),
                        f"D2/spark/{db}")
        barrier(ctx, eng, E, "D2")

    def phase_D3(ctx, eng, E):
        if debug >= 1 and eng == "sync":
            for nm, src in [("t_kpark", k_park), ("t_vpark", v_park), ("t_apark", attn_park),
                            ("t_dpark", den_park), ("t_fpark", ffn_park), ("t_spark", sum_park)]:
                ctx.dma(E, "qs", dbg_taps[nm], src, f"DBG/{nm}")
            for nm, src in [("t_xn", xn), ("t_h", b_h), ("t_rstd", rstd)]:
                ctx.dma(E, "qs", dbg_taps[nm], src.ap(), f"DBG/{nm}")
            if debug < 2:
                ctx.alias("DBG/wq40", "DBG/t_rstd")
                ctx.alias("DBG/wqT40", "DBG/t_rstd")
        for q in range(8):
            b = q % 2
            if eng == "sync":
                if q >= 2:
                    ctx.wait(E, f"D3/add/{q-2}")
                if q == 0:
                    for db2 in range(16):
                        ctx.wait(E, f"D2/spark/{db2}")
                ctx.dma(E, "qs", f_s[b].ap(), sum_park[128 * q:128 * (q + 1), :], f"D3/ls/{q}")
                ctx.dma(E, "qs", f_x[b].ap(), dn["xo"][128 * q:128 * (q + 1), :], f"D3/l/{q}")
            if eng == "vector":
                ctx.wait(E, f"D3/ls/{q}")
                ctx.wait(E, f"D3/l/{q}")
                if q >= 2:
                    ctx.wait(E, f"D3/st/{q-2}")
                inst = E.tensor_tensor(f_o[b].ap(), f_s[b].ap(), f_x[b].ap(), ALU.add)
                ctx.mark("dve", inst, f"D3/add/{q}")
            if eng == "sync":
                ctx.wait(E, f"D3/add/{q}")
                ctx.dma(E, "qs", y[128 * q:(q + 1) * 128, :], f_o[b].ap(), f"D3/st/{q}")
        if eng == "sync":
            if debug >= 1:
                ctx.wait(E, "DBG/t_rstd")
            ctx.wait(E, "D3/st/7")

    def run_phases(ctx, eng, E):
        phase_A(ctx, eng, E)
        phase_C(ctx, eng, E)
        phase_B(ctx, eng, E)
        phase_D1(ctx, eng, E)
        phase_D2(ctx, eng, E)
        phase_D3(ctx, eng, E)

    ctx0 = Ctx(True, None, registry)
    for eng in ENGS:
        run_phases(ctx0, eng, _MockEng())

    from contextlib import ExitStack
    with ExitStack() as st:
        sems = {}
        for s in SEM_NAMES:
            sems[s] = st.enter_context(nc.semaphore(s))
        nc.all_engine_barrier()
        ksr = bass.get_kernel_semaphore_range()
        clr = range(ksr.start + 5, ksr.stop)
        nc.gpsimd.dma_reset(clr)
        nc.gpsimd.sem_clear(clr)
        nc.all_engine_barrier()
        blk = st.enter_context(nc.Block())
        ctx1 = Ctx(False, sems, registry)
        blk.sync(lambda E: run_phases(ctx1, "sync", E))
        blk.tensor(lambda E: run_phases(ctx1, "tensor", E))
        blk.vector(lambda E: run_phases(ctx1, "vector", E))
        blk.scalar(lambda E: run_phases(ctx1, "scalar", E))
        blk.gpsimd(lambda E: run_phases(ctx1, "gpsimd", E))
        for s in ctx0.counts:
            assert ctx0.counts[s] == ctx1.counts[s], \
                f"pass mismatch {s}: {ctx0.counts[s]} vs {ctx1.counts[s]}"
    return nc


def kernel(L0_attn_x, L0_attn_Wq, L0_attn_bq, L0_attn_Wk, L0_attn_bk, L0_attn_Wv,
           L0_attn_bv, L0_attn_Wo, L0_attn_bo, L0_ffn_w_gate_up, L0_ffn_w_proj):
    x = np.ascontiguousarray(np.asarray(L0_attn_x, np.float32).reshape(T, D))
    wq = np.ascontiguousarray(np.asarray(L0_attn_Wq, np.float32))
    wk = np.ascontiguousarray(np.asarray(L0_attn_Wk, np.float32))
    wv = np.ascontiguousarray(np.asarray(L0_attn_Wv, np.float32))
    wo = np.ascontiguousarray(np.asarray(L0_attn_Wo, np.float32))
    wgu = np.ascontiguousarray(np.asarray(L0_ffn_w_gate_up, np.float32))
    wp = np.ascontiguousarray(np.asarray(L0_ffn_w_proj, np.float32))
    bq = np.asarray(L0_attn_bq, np.float32)
    bk = np.asarray(L0_attn_bk, np.float32)
    bv = np.asarray(L0_attn_bv, np.float32)
    bo = np.asarray(L0_attn_bo, np.float32)

    ident_bf, ident_f, prope_bf, ones_bf, rope_cos, rope_sin = _host_consts()

    bqs = np.zeros((128, 32), np.float32)
    for j in range(4):
        for i in range(4):
            bqs[0:64, 4 * j + i] = bq[64 * (8 * j + i): 64 * (8 * j + i) + 64]
            bqs[64:128, 4 * j + i] = bq[64 * (8 * j + 4 + i): 64 * (8 * j + 4 + i) + 64]
    bkc = np.ascontiguousarray(bk.reshape(4, 128).T)
    bvc = np.ascontiguousarray(bv.reshape(4, 128).T)
    boc = np.ascontiguousarray(bo.reshape(16, 128).T)

    import os
    nc = build_program(debug=int(os.environ.get("KDEBUG", "0")))
    in_maps = []
    for c in range(NCORES):
        idx = np.r_[c * TC:(c + 1) * TC, 0:c * TC, (c + 1) * TC:T]
        in_maps.append({
            "x": np.ascontiguousarray(x[idx]),
            "xo": np.ascontiguousarray(x[c * TC:(c + 1) * TC]),
            "wq": wq, "wk": wk, "wv": wv, "wo": wo, "wgu": wgu, "wp": wp,
            "bqs": bqs, "bkc": bkc, "bvc": bvc, "boc": boc,
            "ident_bf": ident_bf, "ident_f": ident_f, "prope": prope_bf, "ones": ones_bf,
            "rope_cos_a": np.ascontiguousarray(rope_cos[:, idx]),
            "rope_sin_a": np.ascontiguousarray(rope_sin[:, idx]),
            "rope_cos_o": np.ascontiguousarray(rope_cos[:, c * TC:(c + 1) * TC]),
            "rope_sin_o": np.ascontiguousarray(rope_sin[:, c * TC:(c + 1) * TC]),
        })
    res = run_bass_kernel_spmd(nc, in_maps, core_ids=list(range(NCORES)))
    import os
    if int(os.environ.get("KDEBUG", "0")):
        kernel._taps = res.results
    out = np.concatenate([r["y"] for r in res.results], axis=0)
    return out.reshape(1, T, D)


# version 08
# speedup vs baseline: 1.0040x; 1.0040x over previous
import numpy as np
import ml_dtypes
import concourse.bass as bass
import concourse.mybir as mybir
from concourse.bass_utils import run_bass_kernel_spmd

F32 = mybir.dt.float32
BF16 = mybir.dt.bfloat16
AF = mybir.ActivationFunctionType
ALU = mybir.AluOpType
AX = mybir.AxisListType
BF = ml_dtypes.bfloat16

NCORES = 8
T = 8192
D = 2048
TC = T // NCORES
HD = 64
NKV = 8
NQ = 32
DKV = NKV * HD
HGU = 16384
HFF = 8192
RCONST = float(1.5 * 2 ** 23)
LN_EPS = 1e-5
RMS_EPS = float(np.finfo(np.float32).eps)
ENGS = ("sync", "tensor", "vector", "scalar", "gpsimd")
SBUF_TOP = 229344


def _host_consts():
    ident_bf = np.eye(128, dtype=BF)
    ident_f = np.eye(128, dtype=np.float32)
    prope = np.zeros((128, 128), np.float32)
    for k in range(128):
        if k % 64 < 32:
            prope[k, k + 32] = 1.0
        else:
            prope[k, k - 32] = -1.0
    ones_bf = np.ones((128, 128), BF)
    inv_freq = (1.0 / (10000.0 ** (np.arange(0, HD, 2, dtype=np.float32) / HD))).astype(np.float32)
    t = np.arange(T, dtype=np.float32)
    ang = t[:, None] * inv_freq[None, :]
    cos64 = np.concatenate([np.cos(ang), np.cos(ang)], axis=1).T.astype(np.float32)
    sin64 = np.concatenate([np.sin(ang), np.sin(ang)], axis=1).T.astype(np.float32)
    rope_cos = np.ascontiguousarray(np.concatenate([cos64, cos64], axis=0))
    sin_fold = sin64.copy()
    sin_fold[0:32, :] *= -1.0
    rope_sin = np.ascontiguousarray(np.concatenate([sin_fold, sin_fold], axis=0))
    return ident_bf, ident_f, prope.astype(BF), ones_bf, rope_cos, rope_sin


class _Dummy:
    def then_inc(self, *a, **k):
        return self


class _MockEng:
    def __getattr__(self, name):
        def f(*a, **k):
            return _Dummy()
        return f


QS_R = 8
QP_R = 4
SEM_NAMES = (["pe", "act", "dve", "pool",
              "bar_sync", "bar_tensor", "bar_vector", "bar_scalar", "bar_gpsimd"]
             + [f"qs{i}" for i in range(QS_R)] + [f"qp{i}" for i in range(QP_R)])


class Ctx:
    def __init__(self, mock, sems, registry):
        self.mock = mock
        self.sems = sems
        self.reg = registry
        self.counts = {s: 0 for s in SEM_NAMES}
        self.qn = {"qs": 0, "qp": 0}

    def dma(self, E, q, dst, src, name=None):
        R = QS_R if q == "qs" else QP_R
        n = self.qn[q]
        self.qn[q] = n + 1
        snm = f"{q}{n % R}"
        if n >= R and not self.mock:
            E.wait_ge(self.sems[snm], 16 * (n // R))
        inst = E.dma_start(dst, src) if not self.mock else _Dummy()
        self.counts[snm] += 16
        if not self.mock:
            inst.then_inc(self.sems[snm], 16)
        assert self.counts[snm] == 16 * (n // R + 1)
        if name is not None and self.mock:
            assert name not in self.reg, f"duplicate event {name}"
            self.reg[name] = (snm, self.counts[snm])
        return inst

    def inc(self, semname, inst, k=1):
        self.counts[semname] += k
        if not self.mock:
            inst.then_inc(self.sems[semname], k)
        return self.counts[semname]

    def mark(self, semname, inst, name=None, k=1):
        c = self.inc(semname, inst, k)
        if name is not None and self.mock:
            assert name not in self.reg, f"duplicate event {name}"
            self.reg[name] = (semname, c)
        return c

    def alias(self, new, old):
        if self.mock:
            self.reg[new] = self.reg[old]

    def wait(self, E, name):
        if self.mock:
            return
        semname, v = self.reg[name]
        E.wait_ge(self.sems[semname], v)

    def fence(self, E, semname, inst):
        c = self.inc(semname, inst)
        if not self.mock:
            E.wait_ge(self.sems[semname], c)


def build_program(debug=0):
    nc = bass.Bass("TRN2", target_bir_lowering=False)

    dn = {}
    def din(name, shape, dt):
        dn[name] = nc.dram_tensor(name, list(shape), dt, kind="ExternalInput").ap()
    din("x", (T, D), F32)
    din("xo", (TC, D), F32)
    din("wq", (D, D), F32)
    din("wk", (DKV, D), F32)
    din("wv", (DKV, D), F32)
    din("wo", (D, D), F32)
    din("wgu", (HGU, D), F32)
    din("wp", (D, HFF), F32)
    din("bqs", (128, 32), F32)
    din("bkc", (128, 4), F32)
    din("bvc", (128, 4), F32)
    din("boc", (128, 16), F32)
    din("ident_bf", (128, 128), BF16)
    din("ident_f", (128, 128), F32)
    din("prope", (128, 128), BF16)
    din("ones", (128, 128), BF16)
    din("rope_cos_a", (128, T), F32)
    din("rope_sin_a", (128, T), F32)
    din("rope_cos_o", (128, TC), F32)
    din("rope_sin_o", (128, TC), F32)

    y = nc.dram_tensor("y", [TC, D], F32, kind="ExternalOutput").ap()

    k_park = nc.dram_tensor("k_park", [DKV, T], BF16, kind="Internal").ap()
    v_park = nc.dram_tensor("v_park", [T, 1024], BF16, kind="Internal").ap()
    attn_park = nc.dram_tensor("attn_park", [D, TC], F32, kind="Internal").ap()
    den_park = nc.dram_tensor("den_park", [D, TC], F32, kind="Internal").ap()
    ffn_park = nc.dram_tensor("ffn_park", [D, TC], F32, kind="Internal").ap()
    sum_park = nc.dram_tensor("sum_park", [TC, D], F32, kind="Internal").ap()
    dbg_taps = {}
    if debug >= 1:
        for nm, shape, dt in [("t_kpark", [DKV, T], BF16), ("t_vpark", [T, 1024], BF16),
                              ("t_apark", [D, TC], F32), ("t_dpark", [D, TC], F32),
                              ("t_fpark", [D, TC], F32), ("t_spark", [TC, D], F32),
                              ("t_xn", [128, 16 * 1024], BF16), ("t_h", [128, 64 * 1024], BF16),
                              ("t_rstd", [128, 1024], F32),
                              ("t_wq40", [128, 2048], BF16), ("t_wqT40", [128, 2048], BF16),
                              ("t_bt", [128, 1024], F32), ("t_bt2", [128, 1024], F32),
                              ("t_sc", [128, 32], F32), ("t_scb", [128, 32], BF16),
                              ("t_rs", [128, 32], F32), ("t_wh", [128, 1024], BF16)]:
            dbg_taps[nm] = nc.dram_tensor(nm, shape, dt, kind="ExternalOutput").ap()

    def alo(name, shape, dt):
        return nc.alloc_sbuf_tensor(name, list(shape), dt)
    xn = alo("xn", (128, 16 * 1024), BF16)
    idb = alo("idb", (128, 128), BF16)
    prp = alo("prp", (128, 128), BF16)
    onb = alo("onb", (128, 128), BF16)
    bqs = alo("s_bqs", (128, 32), F32)
    bkc = alo("s_bkc", (128, 4), F32)
    bvc = alo("s_bvc", (128, 4), F32)
    boc = alo("s_boc", (128, 16), F32)
    rstd = alo("rstd", (128, 1024), F32)
    smA = alo("smA", (128, 16), F32)
    bcol = alo("bcol", (128, 4), F32)
    nkT = alo("nkT", (128, 4), F32)
    nqT = alo("nqT", (128, 4), F32)

    RB = (nc.sbuf_base + 31) & ~31
    assert RB < 72000, f"persistent allocs too large, RB={RB}"

    def at(name, shape, dt, off):
        nbytes = int(np.prod(shape[1:])) * (2 if dt == BF16 else 4)
        assert RB + off + nbytes <= SBUF_TOP, f"{name} overflows SBUF ({RB+off+nbytes})"
        return nc.alloc_sbuf_tensor_at(name, list(shape), dt, offset=RB + off)

    a_x = [at("a_x0", (128, D), F32, 0), at("a_x1", (128, D), F32, 8192)]
    a_xsq = at("a_xsq", (128, D), F32, 16384)
    a_xtm = at("a_xtm", (128, D), BF16, 24576)
    a_xnf = at("a_xnf", (128, 16 * 1024), BF16, 28672)
    a_wkT = at("a_wkT", (128, 16 * 512), BF16, 61440)
    a_wvT = at("a_wvT", (128, 16 * 512), BF16, 77824)
    a_wst = [at("a_wst0", (128, D), BF16, 94208), at("a_wst1", (128, D), BF16, 98304)]
    a_kst = at("a_kst", (128, 4 * 1024), BF16, 102400)
    a_vst = at("a_vst", (128, 8 * 1024), BF16, 110592)
    a_rc = at("a_rc", (128, 1024), F32, 126976)
    a_rs = at("a_rs", (128, 1024), F32, 131072)
    a_vbf = at("a_vbf", (128, 1024), BF16, 137216)
    a_kc = at("a_kc", (128, 1024), F32, 139264)
    a_kc2 = at("a_kc2", (128, 1024), F32, 143360)
    a_kf = at("a_kf", (128, 1024), F32, 147456)
    a_krot = at("a_krot", (128, 1024), F32, 151552)

    c_kp = [at("c_kp0", (128, T), BF16, 0), at("c_kp1", (128, T), BF16, 16384)]
    c_vp = at("c_vp", (128, 64 * 256), BF16, 32768)
    c_wqs = [at("c_wqs0", (128, D), BF16, 65536), at("c_wqs1", (128, D), BF16, 69632)]
    c_wqT = at("c_wqT", (128, 16 * 512), BF16, 73728)
    c_qp = at("c_qp", (128, 4 * 1024), BF16, 90112)
    c_q2 = at("c_q2", (128, 1024), BF16, 100352)
    c_qc = at("c_qc", (128, 1024), F32, 102400)
    c_qc2 = at("c_qc2", (128, 1024), F32, 106496)
    c_e = [at("c_e0", (128, 1024), BF16, 110592), at("c_e1", (128, 1024), BF16, 112640),
           at("c_e2", (128, 1024), BF16, 149504)]
    c_k2 = at("c_k2", (128, 1024), BF16, 114688)
    c_ab = [at("c_ab0", (128, 1024), F32, 116736), at("c_ab1", (128, 1024), F32, 120832)]
    c_dn = [at("c_dn0", (128, 1024), F32, 124928), at("c_dn1", (128, 1024), F32, 129024)]
    rco = at("rco", (128, TC), F32, 133120)
    rso = at("rso", (128, TC), F32, 137216)
    c_qf = at("c_qf", (128, 1024), F32, 141312)
    c_qrot = at("c_qrot", (128, 1024), F32, 145408)

    b_h = at("b_h", (128, 64 * 1024), BF16, 0)
    b_wg = [at("b_wg0", (128, D), BF16, 131072), at("b_wg1", (128, D), BF16, 135168)]
    b_wu = [at("b_wu0", (128, D), BF16, 139264), at("b_wu1", (128, D), BF16, 143360)]
    b_t = at("b_t", (128, 1024), F32, 147456)
    b_wh = at("b_wh", (128, 1024), BF16, 151552)
    b_wqTg = at("b_wqTg", (128, 2048), BF16, 153600)
    b_wqTu = at("b_wqTu", (128, 2048), BF16, 157696)
    b_sg = at("b_sg", (128, 512), F32, 161792)
    b_h2 = at("b_h2", (128, 1024), BF16, 163840)
    b_sc = at("b_sc", (128, 32), F32, 165888)
    b_scb = at("b_scb", (128, 32), BF16, 166016)
    b_rs = at("b_rs", (128, 32), F32, 166144)
    b_t2 = at("b_t2", (128, 1024), F32, 166272)

    d_wps = [at("d_wps0", (128, 2048), BF16, 131072), at("d_wps1", (128, 2048), BF16, 135168)]
    d_t = at("d_t", (128, 1024), F32, 139264)
    d_wh = at("d_wh", (128, 1024), BF16, 143360)
    d_wpT = at("d_wpT", (128, 2048), BF16, 145408)
    d_sbf = at("d_sbf", (128, 1024), F32, 149504)
    d_sc = at("d_sc", (128, 32), F32, 153600)
    d_scb = at("d_scb", (128, 32), BF16, 153728)
    d_rs = at("d_rs", (128, 32), F32, 153856)
    d_t2 = at("d_t2", (128, 1024), F32, 154112)

    e_attn = at("e_attn", (128, 16 * 1024), BF16, 0)
    e_ab = at("e_ab", (128, 1024), F32, 32768)
    e_dn = at("e_dn", (128, 1024), F32, 36864)
    e_rn = at("e_rn", (128, 1024), F32, 40960)
    e_wos = [at("e_wos0", (128, D), BF16, 45056), at("e_wos1", (128, D), BF16, 49152)]
    e_woT = at("e_woT", (128, 2048), BF16, 53248)
    e_so = at("e_so", (128, 1024), F32, 57344)
    e_ssum = at("e_ssum", (128, 1024), F32, 61440)
    e_ft = [at("e_ft0", (128, 1024), F32, 65536), at("e_ft1", (128, 1024), F32, 69632)]
    e_tm = at("e_tm", (128, 1024), F32, 73728)
    idf = at("idf", (128, 128), F32, 77824)

    f_s = [at("f_s0", (128, D), F32, 0), at("f_s1", (128, D), F32, 8192)]
    f_x = [at("f_x0", (128, D), F32, 16384), at("f_x1", (128, D), F32, 24576)]
    f_o = [at("f_o0", (128, D), F32, 32768), at("f_o1", (128, D), F32, 40960)]

    P = [nc.alloc_psum_tensor(f"P{i}", [128, 1024], F32) for i in range(4)]

    registry = {}

    def P3b():
        return P[3].ap().bitcast(BF16)

    def tp16(E, src_sb, idb_ap):
        p3 = P3b()
        inst = None
        for dc in range(16):
            half, slot = divmod(dc, 8)
            inst = E.transpose(p3[:, half * 1024 + slot * 128: half * 1024 + slot * 128 + 128],
                               src_sb[:, dc * 128:(dc + 1) * 128], idb_ap)
        return inst

    def barrier(ctx, eng, E, ph):
        inst = E.nop() if not ctx.mock else _Dummy()
        ctx.mark("bar_" + eng, inst, f"bar/{ph}/{eng}")
        for o in ENGS:
            if o != eng:
                ctx.wait(E, f"bar/{ph}/{o}")

    def phase_A(ctx, eng, E):
        if eng == "sync":
            for wi in range(40):
                ctx.dma(E, "qs", idb.ap(), dn["ident_bf"], f"A/warm/{wi}")
                ctx.wait(E, f"A/warm/{wi}")
        if eng == "sync":
            for nm, dst in [("ident_bf", idb),
                            ("prope", prp), ("ones", onb), ("bqs", bqs),
                            ("bkc", bkc), ("bvc", bvc), ("boc", boc)]:
                ctx.dma(E, "qs", dst.ap(), dn[nm], f"A/ld/{nm}")
        if eng in ("tensor", "scalar", "vector", "gpsimd"):
            for nm in ("ident_bf", "prope", "ones", "bqs", "bkc", "bvc", "boc"):
                ctx.wait(E, f"A/ld/{nm}")
        for u in range(8):
            wi, r = divmod(u, 4)
            wsrc = "wk" if wi == 0 else "wv"
            wT = a_wkT if wi == 0 else a_wvT
            b = u % 2
            if eng == "gpsimd":
                if u >= 2:
                    ctx.wait(E, f"A/wtp/{u-2}")
                ctx.dma(E, "qp", a_wst[b].ap(), dn[wsrc][r * 128:(r + 1) * 128, :], f"A/wld/{u}")
            if eng == "tensor":
                if u == 0:
                    ctx.wait(E, "A/ld/ident_bf")
                ctx.wait(E, f"A/wld/{u}")
                if u > 0:
                    ctx.wait(E, f"A/wev/{u-1}")
                inst = tp16(E, a_wst[b].ap(), idb.ap())
                ctx.mark("pe", inst, f"A/wtp/{u}")
            if eng == "vector":
                ctx.wait(E, f"A/wtp/{u}")
                p3 = P3b()
                inst = None
                for h in range(2):
                    src = p3[:, h * 1024:(h + 1) * 1024].rearrange("p (s j) -> p s j", j=128)
                    dst = wT.ap().rearrange("p (dc rr j) -> p dc rr j", rr=4, j=128)[
                        :, h * 8:(h + 1) * 8, r, :]
                    inst = E.tensor_copy(dst, src)
                ctx.mark("dve", inst, f"A/wev/{u}")

        seq_base = 0
        for c in range(8):
            xdst = xn if c == 0 else a_xnf
            s_order = [2, 3, 2, 3, 4, 5, 6, 7, 0, 1] if c == 0 else list(range(8))
            for s_i, s in enumerate(s_order):
                n = 8 * c + s
                seq = seq_base + s_i
                b = seq % 2
                if eng == "sync":
                    if seq >= 2:
                        ctx.wait(E, f"A/ap/{seq-2}")
                    ctx.dma(E, "qs", a_x[b].ap(), dn["x"][n * 128:(n + 1) * 128, :], f"A/x/{seq}")
                if eng == "scalar":
                    ctx.wait(E, f"A/x/{seq}")
                    if seq > 0:
                        ctx.wait(E, f"A/st/{seq-1}")
                    inst = E.activation(a_xsq.ap(), a_x[b].ap(), AF.Square)
                    ctx.mark("act", inst, f"A/sq/{seq}")
                    ctx.wait(E, f"A/var/{seq}")
                    inst = E.activation(smA.ap()[:, 4:5], smA.ap()[:, 3:4], AF.Sqrt)
                    ctx.mark("act", inst, f"A/sd/{seq}")
                    ctx.wait(E, f"A/st/{seq}")
                    if seq > 0:
                        ctx.wait(E, f"A/tp/{seq-1}")
                    inst = E.activation(a_xtm.ap(), a_x[b].ap(), AF.Identity,
                                        bias=smA.ap()[:, 5:6], scale=smA.ap()[:, 6:7])
                    ctx.mark("act", inst, f"A/ap/{seq}")
                if eng == "vector":
                    ctx.wait(E, f"A/x/{seq}")
                    E.tensor_reduce(smA.ap()[:, 0:1].unsqueeze(2),
                                    a_x[b].ap().unsqueeze(1), AX.X, ALU.add)
                    ctx.wait(E, f"A/sq/{seq}")
                    inst = E.tensor_reduce(smA.ap()[:, 1:2].unsqueeze(2),
                                           a_xsq.ap().unsqueeze(1), AX.X, ALU.add)
                    ctx.fence(E, "dve", inst)
                    E.tensor_scalar(smA.ap()[:, 2:3], smA.ap()[:, 0:1], 1.0 / D, None, ALU.mult)
                    E.tensor_tensor(smA.ap()[:, 3:4], smA.ap()[:, 2:3], smA.ap()[:, 2:3], ALU.mult)
                    E.tensor_scalar(smA.ap()[:, 3:4], smA.ap()[:, 3:4], LN_EPS, None, ALU.subtract)
                    inst = E.scalar_tensor_tensor(smA.ap()[:, 3:4], smA.ap()[:, 1:2], 1.0 / D,
                                                  smA.ap()[:, 3:4], ALU.mult, ALU.subtract)
                    ctx.mark("dve", inst, f"A/var/{seq}")
                    ctx.wait(E, f"A/sd/{seq}")
                    E.reciprocal(smA.ap()[:, 6:7], smA.ap()[:, 4:5])
                    inst = E.scalar_tensor_tensor(smA.ap()[:, 5:6], smA.ap()[:, 2:3], -1.0,
                                                  smA.ap()[:, 6:7], ALU.mult, ALU.mult)
                    ctx.mark("dve", inst, f"A/st/{seq}")
                if eng == "tensor":
                    ctx.wait(E, f"A/ap/{seq}")
                    if seq == 0:
                        ctx.wait(E, "A/wev/7")
                    else:
                        ctx.wait(E, f"A/xe/{seq-1}")
                    if s_i == 0 and c > 0:
                        ctx.wait(E, f"A/vev/{c-1}/3")
                    inst = tp16(E, a_xtm.ap(), idb.ap())
                    ctx.mark("pe", inst, f"A/tp/{seq}")
                if eng == "vector":
                    ctx.wait(E, f"A/tp/{seq}")
                    p3 = P3b()
                    inst = None
                    for h in range(2):
                        src = p3[:, h * 1024:(h + 1) * 1024].rearrange("p (ss j) -> p ss j", j=128)
                        dst = xdst.ap().rearrange("p (dc t) -> p dc t", t=1024)[
                            :, h * 8:(h + 1) * 8, s * 128:(s + 1) * 128]
                        inst = E.tensor_copy(dst, src)
                    ctx.mark("dve", inst, f"A/xe/{seq}")

            if eng == "sync":
                if c > 0:
                    ctx.wait(E, f"A/krope/{c-1}/3")
                ctx.dma(E, "qs", a_rc.ap(), dn["rope_cos_a"][:, c * 1024:(c + 1) * 1024],
                        f"A/rtc/{c}")
                ctx.dma(E, "qs", a_rs.ap(), dn["rope_sin_a"][:, c * 1024:(c + 1) * 1024],
                        f"A/rt/{c}")

            seq_base += len(s_order)
            for ob in range(4):
                u = 4 * c + ob
                if eng == "tensor":
                    if ob == 0:
                        ctx.wait(E, f"A/xe/{seq_base-1}")
                    if u > 0:
                        ctx.wait(E, f"A/ke/{(u-1)//4}/{(u-1)%4}")
                    inst = None
                    for half in range(2):
                        for dc in range(16):
                            inst = E.matmul(P[0].ap()[:, half * 512:(half + 1) * 512],
                                            a_wkT.ap()[:, dc * 512 + ob * 128: dc * 512 + (ob + 1) * 128],
                                            xdst.ap()[:, dc * 1024 + half * 512: dc * 1024 + (half + 1) * 512],
                                            start=(dc == 0), stop=(dc == 15))
                    ctx.mark("pe", inst, f"A/pk/{c}/{ob}")
                    if u > 0:
                        ctx.wait(E, f"A/vevac/{(u-1)//4}/{(u-1)%4}")
                    for half in range(2):
                        for dc in range(16):
                            inst = E.matmul(P[1].ap()[:, half * 512:(half + 1) * 512],
                                            a_wvT.ap()[:, dc * 512 + ob * 128: dc * 512 + (ob + 1) * 128],
                                            xdst.ap()[:, dc * 1024 + half * 512: dc * 1024 + (half + 1) * 512],
                                            start=(dc == 0), stop=(dc == 15))
                    ctx.mark("pe", inst, f"A/pv/{c}/{ob}")
                if eng == "scalar":
                    ctx.wait(E, f"A/pk/{c}/{ob}")
                    if u > 0:
                        ctx.wait(E, f"A/krope/{(u-1)//4}/{(u-1)%4}")
                        ctx.wait(E, f"A/krot/{(u-1)//4}/{(u-1)%4}")
                    inst = E.activation(a_kf.ap(), P[0].ap(), AF.Identity,
                                        bias=bkc.ap()[:, ob:ob + 1])
                    ctx.mark("act", inst, f"A/ke/{c}/{ob}")
                    ctx.wait(E, f"A/pv/{c}/{ob}")
                    if u > 0:
                        ctx.wait(E, f"A/vtp/{(u-1)//4}/{(u-1)%4}")
                    inst = E.activation(a_vbf.ap(), P[1].ap(), AF.Identity,
                                        bias=bvc.ap()[:, ob:ob + 1])
                    ctx.mark("act", inst, f"A/vevac/{c}/{ob}")
                if eng == "sync":
                    ctx.wait(E, f"A/ke/{c}/{ob}")
                    if u > 0:
                        ctx.wait(E, f"A/krope/{(u-1)//4}/{(u-1)%4}")
                    ctx.dma(E, "qs", a_krot.ap()[0:32, :], a_kf.ap()[32:64, :])
                    ctx.dma(E, "qs", a_krot.ap()[32:64, :], a_kf.ap()[0:32, :])
                    ctx.dma(E, "qs", a_krot.ap()[64:96, :], a_kf.ap()[96:128, :])
                    ctx.dma(E, "qs", a_krot.ap()[96:128, :], a_kf.ap()[64:96, :],
                            f"A/krot/{c}/{ob}")
                if eng == "tensor":
                    ctx.wait(E, f"A/vevac/{c}/{ob}")
                    if u > 0:
                        ctx.wait(E, f"A/vev/{(u-1)//4}/{(u-1)%4}")
                    p3 = P3b()
                    for bb in range(8):
                        inst = E.transpose(p3[:, 1024 + bb * 128: 1024 + (bb + 1) * 128],
                                           a_vbf.ap()[:, bb * 128:(bb + 1) * 128], idb.ap())
                    ctx.mark("pe", inst, f"A/vtp/{c}/{ob}")
                if eng == "vector":
                    ctx.wait(E, f"A/krot/{c}/{ob}")
                    if ob == 0:
                        ctx.wait(E, f"A/rtc/{c}")
                        ctx.wait(E, f"A/rt/{c}")
                        if c > 0:
                            ctx.wait(E, f"A/kpark/{c-1}")
                    E.tensor_tensor(a_kc.ap(), a_kf.ap(), a_rc.ap(), ALU.mult)
                    E.tensor_tensor(a_kc2.ap(), a_krot.ap(), a_rs.ap(), ALU.mult)
                    inst = E.tensor_tensor(a_kst.ap()[:, ob * 1024:(ob + 1) * 1024],
                                           a_kc.ap(), a_kc2.ap(), ALU.add)
                    ctx.mark("dve", inst, f"A/krope/{c}/{ob}")
                    ctx.wait(E, f"A/vtp/{c}/{ob}")
                    if ob == 0:
                        if c > 0:
                            ctx.wait(E, f"A/vpark/{c-1}")
                        E.memset(a_vst.ap().rearrange("p (bb q j) -> p bb q j", q=8, j=128)[
                            :, :, :, 64:128], 1.0)
                    p3 = P3b()
                    src = p3[:, 1024:2048].rearrange("p (bb g j) -> p bb g j", g=2, j=64)
                    dst = a_vst.ap().rearrange("p (bb q j) -> p bb q j", q=8, j=128)[
                        :, :, 2 * ob:2 * ob + 2, 0:64]
                    inst = E.tensor_copy(dst, src)
                    ctx.mark("dve", inst, f"A/vev/{c}/{ob}")
            if eng == "sync":
                ctx.wait(E, f"A/krope/{c}/3")
                dst = k_park[:, c * 1024:(c + 1) * 1024].rearrange("(o p) t -> p o t", p=128)
                ctx.dma(E, "qs", dst, a_kst.ap().rearrange("p (o t) -> p o t", t=1024),
                        f"A/kpark/{c}")
                ctx.wait(E, f"A/vev/{c}/3")
                dst = v_park[c * 1024:(c + 1) * 1024, :].rearrange("(bb p) j -> p bb j", p=128)
                ctx.dma(E, "qs", dst, a_vst.ap().rearrange("p (bb j) -> p bb j", j=1024),
                        f"A/vpark/{c}")
        barrier(ctx, eng, E, "A")

    def phase_C(ctx, eng, E):
        if eng == "sync":
            ctx.dma(E, "qs", rco.ap(), dn["rope_cos_o"], "C/ropeoc")
            ctx.dma(E, "qs", rso.ap(), dn["rope_sin_o"], "C/ropeo")
        for j in range(4):
            kb = j % 2
            lastg = 8 * j - 1
            if eng == "sync":
                if j >= 2:
                    ctx.wait(E, f"C/pdone/{j-2}")
                if j == 0:
                    for cc in range(8):
                        ctx.wait(E, f"A/kpark/{cc}")
                        ctx.wait(E, f"A/vpark/{cc}")
                ctx.dma(E, "qs", c_kp[kb].ap(), k_park[128 * j:128 * (j + 1), :], f"C/kp/{j}")
                if j >= 1:
                    ctx.wait(E, f"C/pdone/{j-1}")
                src = v_park[:, 256 * j:256 * (j + 1)].rearrange("(bb p) i -> p bb i", p=128)
                ctx.dma(E, "qs", c_vp.ap().rearrange("p (bb i) -> p bb i", i=256), src,
                        f"C/vp/{j}")
            for r in range(4):
                wb = (4 * j + r) % 2
                if eng == "gpsimd":
                    idx = 4 * j + r
                    if idx >= 2:
                        ctx.wait(E, f"C/wqtp/{(idx-2)//4}/{(idx-2)%4}")
                    ctx.dma(E, "qp", c_wqs[wb].ap(),
                            dn["wq"][512 * j + 128 * r: 512 * j + 128 * (r + 1), :],
                            f"C/wq/{j}/{r}")
                if eng == "tensor":
                    ctx.wait(E, f"C/wq/{j}/{r}")
                    if 4 * j + r > 0:
                        pj, pr = (j, r - 1) if r > 0 else (j - 1, 3)
                        ctx.wait(E, f"C/wqev/{pj}/{pr}")
                    inst = tp16(E, c_wqs[wb].ap(), idb.ap())
                    ctx.mark("pe", inst, f"C/wqtp/{j}/{r}")
                if eng == "vector":
                    ctx.wait(E, f"C/wqtp/{j}/{r}")
                    if r == 0 and j > 0:
                        ctx.wait(E, f"C/qmm/{j-1}/3")
                    p3 = P3b()
                    inst = None
                    for h in range(2):
                        for g in range(2):
                            m = 2 * r + g
                            qg = 2 * (m % 4) + (m // 4)
                            src = p3[:, h * 1024:(h + 1) * 1024].rearrange(
                                "p (ss gg i) -> p ss gg i", gg=2, i=64)[:, :, g, :]
                            dst = c_wqT.ap().rearrange("p (dc qq i) -> p dc qq i", qq=8, i=64)[
                                :, h * 8:(h + 1) * 8, qg, :]
                            inst = E.tensor_copy(dst, src)
                    ctx.mark("dve", inst, f"C/wqev/{j}/{r}")
            for cc in range(8):
                if eng == "scalar":
                    if cc == 0:
                        ctx.wait(E, f"C/kp/{j}")
                    else:
                        ctx.wait(E, f"C/k2mm/{j}/{cc-1}")
                    inst = E.activation(c_k2.ap(), c_kp[kb].ap()[:, cc * 1024:(cc + 1) * 1024],
                                        AF.Square)
                    ctx.mark("act", inst, f"C/k2/{j}/{cc}")
                if eng == "tensor":
                    ctx.wait(E, f"C/k2/{j}/{cc}")
                    if cc == 0:
                        ctx.wait(E, f"C/wqev/{j}/3")
                    else:
                        ctx.wait(E, f"C/k2red/{j}/{cc-1}")
                    inst = None
                    for half in range(2):
                        inst = E.matmul(P[3].ap()[:, half * 512:(half + 1) * 512], onb.ap(),
                                        c_k2.ap()[:, half * 512:(half + 1) * 512],
                                        start=True, stop=True)
                    ctx.mark("pe", inst, f"C/k2mm/{j}/{cc}")
                if eng == "vector":
                    ctx.wait(E, f"C/k2mm/{j}/{cc}")
                    if cc == 0 and j > 0:
                        ctx.wait(E, f"C/nk/{j-1}")
                    inst = E.tensor_reduce(nkT.ap()[:, 2:3].unsqueeze(2), P[3].ap().unsqueeze(1),
                                           AX.X, ALU.max)
                    ctx.fence(E, "dve", inst)
                    if cc == 0:
                        inst = E.tensor_copy(nkT.ap()[:, 3:4], nkT.ap()[:, 2:3])
                    else:
                        inst = E.tensor_tensor(nkT.ap()[:, 3:4], nkT.ap()[:, 3:4],
                                               nkT.ap()[:, 2:3], ALU.max)
                    ctx.mark("dve", inst, f"C/k2red/{j}/{cc}")
            if eng == "scalar":
                ctx.wait(E, f"C/k2red/{j}/7")
                inst = E.activation(nkT.ap()[:, 0:1], nkT.ap()[:, 3:4], AF.Sqrt)
                ctx.mark("act", inst, f"C/nk/{j}")

            for i in range(4):
                if eng == "tensor":
                    if i == 0:
                        ctx.wait(E, f"C/wqev/{j}/3")
                        ctx.wait(E, f"C/k2red/{j}/7")
                    else:
                        ctx.wait(E, f"C/qev/{j}/{i-1}")
                    inst = None
                    for half in range(2):
                        for dc in range(16):
                            inst = E.matmul(P[3].ap()[:, half * 512:(half + 1) * 512],
                                            c_wqT.ap()[:, dc * 512 + i * 128: dc * 512 + i * 128 + 128],
                                            xn.ap()[:, dc * 1024 + half * 512: dc * 1024 + (half + 1) * 512],
                                            start=(dc == 0), stop=(dc == 15))
                    ctx.mark("pe", inst, f"C/qmm/{j}/{i}")
                if eng == "scalar":
                    ctx.wait(E, f"C/qmm/{j}/{i}")
                    if 4 * j + i > 0:
                        pj, pi = (j, i - 1) if i > 0 else (j - 1, 3)
                        ctx.wait(E, f"C/qrope/{pj}/{pi}")
                        ctx.wait(E, f"C/qrot/{pj}/{pi}")
                    inst = E.activation(c_qf.ap(), P[3].ap(), AF.Identity,
                                        bias=bqs.ap()[:, 4 * j + i: 4 * j + i + 1])
                    ctx.mark("act", inst, f"C/qev/{j}/{i}")
                if eng == "sync":
                    ctx.wait(E, f"C/qev/{j}/{i}")
                    if 4 * j + i > 0:
                        pj, pi = (j, i - 1) if i > 0 else (j - 1, 3)
                        ctx.wait(E, f"C/qrope/{pj}/{pi}")
                    ctx.dma(E, "qs", c_qrot.ap()[0:32, :], c_qf.ap()[32:64, :])
                    ctx.dma(E, "qs", c_qrot.ap()[32:64, :], c_qf.ap()[0:32, :])
                    ctx.dma(E, "qs", c_qrot.ap()[64:96, :], c_qf.ap()[96:128, :])
                    ctx.dma(E, "qs", c_qrot.ap()[96:128, :], c_qf.ap()[64:96, :],
                            f"C/qrot/{j}/{i}")
                if eng == "vector":
                    ctx.wait(E, f"C/qrot/{j}/{i}")
                    if i == 0 and j == 0:
                        ctx.wait(E, "C/ropeoc")
                        ctx.wait(E, "C/ropeo")
                    if i == 0 and j > 0:
                        ctx.wait(E, f"C/sc/{lastg}/63")
                    E.tensor_tensor(c_qc.ap(), c_qf.ap(), rco.ap(), ALU.mult)
                    E.tensor_tensor(c_qc2.ap(), c_qrot.ap(), rso.ap(), ALU.mult)
                    inst = E.tensor_tensor(c_qp.ap()[:, i * 1024:(i + 1) * 1024],
                                           c_qc.ap(), c_qc2.ap(), ALU.add)
                    ctx.mark("dve", inst, f"C/qrope/{j}/{i}")
                if eng == "scalar":
                    ctx.wait(E, f"C/qrope/{j}/{i}")
                    if i > 0:
                        ctx.wait(E, f"C/q2mm/{j}/{i-1}")
                    inst = E.activation(c_q2.ap(), c_qp.ap()[:, i * 1024:(i + 1) * 1024], AF.Square)
                    ctx.mark("act", inst, f"C/q2/{j}/{i}")
                if eng == "tensor":
                    ctx.wait(E, f"C/q2/{j}/{i}")
                    if i > 0:
                        ctx.wait(E, f"C/q2red/{j}/{i-1}")
                    elif j > 0:
                        ctx.wait(E, f"C/expall/{lastg}")
                    inst = None
                    for half in range(2):
                        inst = E.matmul(P[1].ap()[:, half * 512:(half + 1) * 512], onb.ap(),
                                        c_q2.ap()[:, half * 512:(half + 1) * 512],
                                        start=True, stop=True)
                    ctx.mark("pe", inst, f"C/q2mm/{j}/{i}")
                if eng == "vector":
                    ctx.wait(E, f"C/q2mm/{j}/{i}")
                    if i > 0:
                        ctx.wait(E, f"C/nq/{j}/{i-1}")
                    inst = E.tensor_reduce(nqT.ap()[:, 0:1].unsqueeze(2), P[1].ap().unsqueeze(1),
                                           AX.X, ALU.max)
                    ctx.mark("dve", inst, f"C/q2red/{j}/{i}")
                if eng == "scalar":
                    ctx.wait(E, f"C/q2red/{j}/{i}")
                    inst = E.activation(nqT.ap()[:, 1:2], nqT.ap()[:, 0:1], AF.Sqrt)
                    ctx.mark("act", inst, f"C/nq/{j}/{i}")
                if eng == "vector":
                    ctx.wait(E, f"C/nq/{j}/{i}")
                    ctx.wait(E, f"C/nk/{j}")
                    if i == 0 and j > 0:
                        ctx.wait(E, f"C/expall/{lastg}")
                    inst = E.scalar_tensor_tensor(bcol.ap()[:, i:i + 1], nqT.ap()[:, 1:2],
                                                  -0.125, nkT.ap()[:, 0:1], ALU.mult, ALU.mult)
                    ctx.mark("dve", inst, f"C/bh/{j}/{i}")

            for m in range(8):
                i, p = divmod(m, 2)
                gh = 8 * j + m
                hh = 8 * j + 4 * p + i
                if eng == "tensor":
                    if m == 0:
                        ctx.wait(E, f"C/vp/{j}")
                        ctx.wait(E, f"C/bh/{j}/3")
                    inst = None
                    for tkb in range(64):
                        sb_i = (0, 1, 3)[tkb % 3]
                        if tkb >= 3:
                            ctx.wait(E, f"C/exp/{gh}/{tkb-3}")
                        elif gh > 0:
                            ctx.wait(E, f"C/expall/{gh-1}")
                        for half in range(2):
                            inst = E.matmul(
                                P[sb_i].ap()[:, half * 512:(half + 1) * 512],
                                c_kp[kb].ap()[64 * p:64 * (p + 1), tkb * 128:(tkb + 1) * 128],
                                c_qp.ap()[64 * p:64 * (p + 1),
                                          i * 1024 + half * 512: i * 1024 + half * 512 + 512],
                                start=True, stop=True)
                        ctx.mark("pe", inst, f"C/sc/{gh}/{tkb}")
                        ctx.wait(E, f"C/exp/{gh}/{tkb}")
                        if tkb == 0 and gh > 0:
                            ctx.wait(E, f"C/aev/{gh-1}")
                            ctx.wait(E, f"C/dev/{gh-1}")
                        for half in range(2):
                            inst = E.matmul(
                                P[2].ap()[:, half * 512:(half + 1) * 512],
                                c_vp.ap()[:, tkb * 256 + p * 128: tkb * 256 + (p + 1) * 128],
                                c_e[tkb % 3].ap()[:, half * 512:(half + 1) * 512],
                                start=(tkb == 0), stop=(tkb == 63))
                        cnum = ctx.mark("pe", inst, f"C/av/{gh}/{tkb}")
                        if tkb == 63 and ctx.mock:
                            registry[f"C/avall/{gh}"] = ("pe", cnum)
                if eng == "scalar":
                    for tkb in range(64):
                        ctx.wait(E, f"C/sc/{gh}/{tkb}")
                        if tkb >= 3:
                            ctx.wait(E, f"C/av/{gh}/{tkb-3}")
                        elif gh > 0:
                            ctx.wait(E, f"C/avall/{gh-1}")
                        inst = E.activation(c_e[tkb % 3].ap(), P[(0, 1, 3)[tkb % 3]].ap(), AF.Exp,
                                            bias=bcol.ap()[:, i:i + 1], scale=0.125)
                        cnum = ctx.mark("act", inst, f"C/exp/{gh}/{tkb}")
                        if tkb == 63 and ctx.mock:
                            registry[f"C/expall/{gh}"] = ("act", cnum)
                    ctx.wait(E, f"C/av/{gh}/63")
                    if gh >= 2:
                        ctx.wait(E, f"C/apark/{gh-2}")
                    inst = E.activation(c_ab[gh % 2].ap()[0:64, :], P[2].ap()[0:64, :],
                                        AF.Identity)
                    ctx.mark("act", inst, f"C/aev/{gh}")
                if eng == "vector":
                    ctx.wait(E, f"C/av/{gh}/63")
                    if gh >= 2:
                        ctx.wait(E, f"C/dpark/{gh-2}")
                    inst = E.tensor_copy(c_dn[gh % 2].ap()[64:128, :], P[2].ap()[64:128, :])
                    ctx.mark("dve", inst, f"C/dev/{gh}")
                if eng == "sync":
                    ctx.wait(E, f"C/aev/{gh}")
                    ctx.dma(E, "qs", attn_park[64 * hh:64 * (hh + 1), :],
                            c_ab[gh % 2].ap()[0:64, :], f"C/apark/{gh}")
                    ctx.wait(E, f"C/dev/{gh}")
                    ctx.dma(E, "qs", den_park[64 * hh:64 * (hh + 1), :],
                            c_dn[gh % 2].ap()[64:128, :], f"C/dpark/{gh}")
            if eng == "sync" and ctx.mock:
                registry[f"C/pdone/{j}"] = registry[f"C/dpark/{8*j+7}"]
        barrier(ctx, eng, E, "C")

    def quant_tile(ctx, eng, E, tag, wtile, W, sc, scb, rs, tbuf, tbuf2, whbuf):
        ng = W // 64
        nh = W // 1024
        if eng == "vector":
            inst = E.tensor_reduce(sc.ap()[:, 0:ng].unsqueeze(2),
                                   wtile.rearrange("p (g e) -> p g e", e=64),
                                   AX.X, ALU.add, apply_absolute_value=True)
            ctx.fence(E, "dve", inst)
            inst = E.tensor_scalar(scb.ap()[:, 0:ng], sc.ap()[:, 0:ng], 1.0 / 64, 1e-8,
                                   ALU.mult, ALU.max)
            ctx.mark("dve", inst, f"{tag}/scb")
        if eng == "scalar":
            ctx.wait(E, f"{tag}/scb")
            inst = E.activation(sc.ap()[:, 0:ng], scb.ap()[:, 0:ng], AF.Identity)
            ctx.mark("act", inst, f"{tag}/scf")
        if eng == "vector":
            ctx.wait(E, f"{tag}/scf")
            inst = E.reciprocal(rs.ap()[:, 0:ng], sc.ap()[:, 0:ng])
            ctx.mark("dve", inst, f"{tag}/rs")
        for h in range(nh):
            wsl = wtile[:, h * 1024:(h + 1) * 1024].rearrange("p (g e) -> p g e", e=64)
            rsb = rs.ap()[:, h * 16:(h + 1) * 16].unsqueeze(2).broadcast_to([128, 16, 64])
            scbb = sc.ap()[:, h * 16:(h + 1) * 16].unsqueeze(2).broadcast_to([128, 16, 64])
            if eng == "vector":
                E.tensor_tensor(tbuf.ap().rearrange("p (g e) -> p g e", e=64),
                                wsl, rsb, ALU.mult)
                E.tensor_scalar_add(tbuf2.ap(), tbuf.ap(), RCONST)
                E.tensor_scalar_sub(tbuf.ap(), tbuf2.ap(), RCONST)
                E.tensor_scalar(tbuf2.ap(), tbuf.ap(), -1.0, 1.0, ALU.max, ALU.min)
                E.tensor_tensor(whbuf.ap().rearrange("p (g e) -> p g e", e=64),
                                tbuf2.ap().rearrange("p (g e) -> p g e", e=64), scbb, ALU.mult)
                E.tensor_tensor(whbuf.ap(), whbuf.ap(), wtile[:, h * 1024:(h + 1) * 1024],
                                ALU.subtract)
                E.tensor_tensor(wtile[:, h * 1024:(h + 1) * 1024],
                                wtile[:, h * 1024:(h + 1) * 1024], whbuf.ap(), ALU.add)
                inst = E.tensor_scalar_add(rs.ap()[:, 0:1], rs.ap()[:, 0:1], 0.0)
                ctx.mark("dve", inst, f"{tag}/w/{h}")
        if eng == "vector":
            ctx.alias(f"{tag}/done", f"{tag}/w/{nh-1}")

    def phase_B(ctx, eng, E):
        for k in range(64):
            wb = k % 2
            if eng == "gpsimd":
                if k >= 2:
                    ctx.wait(E, f"B/tpg/{k-2}")
                    ctx.wait(E, f"B/tpu/{k-2}")
                ctx.dma(E, "qp", b_wg[wb].ap(), dn["wgu"][128 * k:128 * (k + 1), :], f"B/lg/{k}")
                ctx.dma(E, "qp", b_wu[wb].ap(),
                        dn["wgu"][HFF + 128 * k: HFF + 128 * (k + 1), :], f"B/lu/{k}")
            kq_list = [0, 1] if k == 0 else ([k + 1] if k + 1 < 64 else [])
            for kq in kq_list:
                wbq = kq % 2
                if eng == "vector":
                    ctx.wait(E, f"B/lg/{kq}")
                if eng in ("vector", "gpsimd", "scalar"):
                    quant_tile(ctx, eng, E, f"B/qg/{kq}", b_wg[wbq].ap(), D,
                               b_sc, b_scb, b_rs, b_t, b_t2, b_wh)
                if eng == "vector":
                    ctx.wait(E, f"B/lu/{kq}")
                if eng in ("vector", "gpsimd", "scalar"):
                    quant_tile(ctx, eng, E, f"B/qu/{kq}", b_wu[wbq].ap(), D,
                               b_sc, b_scb, b_rs, b_t, b_t2, b_wh)
            if debug >= 2 and k == 40 and eng == "sync":
                ctx.wait(E, "B/qg/40/done")
                ctx.dma(E, "qs", dbg_taps["t_wq40"], b_wg[0].ap(), "DBG/wq40")
                for nm, src in [("t_bt", b_t), ("t_bt2", b_t2), ("t_sc", b_sc),
                                ("t_scb", b_scb), ("t_rs", b_rs), ("t_wh", b_wh)]:
                    ctx.dma(E, "qs", dbg_taps[nm], src.ap(), f"DBG/{nm}")
            if debug >= 2 and k == 40 and eng == "vector":
                ctx.wait(E, "DBG/wq40")
                inst = E.nop()
                ctx.mark("dve", inst, "DBG/wq40ok")
            if debug >= 2 and k == 42 and eng == "gpsimd":
                ctx.wait(E, "DBG/wq40ok")
            if eng == "tensor":
                ctx.wait(E, f"B/qg/{k}/done")
                if debug >= 2 and k == 40:
                    ctx.wait(E, "DBG/wq40ok")
                if k > 0:
                    ctx.wait(E, f"B/evu/{k-1}")
                inst = tp16(E, b_wg[wb].ap(), idb.ap())
                ctx.mark("pe", inst, f"B/tpg/{k}")
            if eng == "vector":
                ctx.wait(E, f"B/tpg/{k}")
                if k > 0:
                    ctx.wait(E, f"B/mmg/{k-1}")
                p3 = P3b()
                inst = None
                for h in range(2):
                    inst = E.tensor_copy(b_wqTg.ap()[:, h * 1024:(h + 1) * 1024],
                                         p3[:, h * 1024:(h + 1) * 1024])
                ctx.mark("dve", inst, f"B/evg/{k}")
            if debug >= 2 and k == 40 and eng == "sync":
                ctx.wait(E, "B/evg/40")
                ctx.dma(E, "qs", dbg_taps["t_wqT40"], b_wqTg.ap(), "DBG/wqT40")
            if debug >= 2 and k == 41 and eng == "tensor":
                ctx.wait(E, "DBG/wqT40")
            if eng == "tensor":
                ctx.wait(E, f"B/evg/{k}")
                ctx.wait(E, f"B/qu/{k}/done")
                inst = tp16(E, b_wu[wb].ap(), idb.ap())
                ctx.mark("pe", inst, f"B/tpu/{k}")
            if eng == "vector":
                ctx.wait(E, f"B/tpu/{k}")
                if k > 0:
                    ctx.wait(E, f"B/mmu/{k-1}")
                p3 = P3b()
                inst = None
                for h in range(2):
                    inst = E.tensor_copy(b_wqTu.ap()[:, h * 1024:(h + 1) * 1024],
                                         p3[:, h * 1024:(h + 1) * 1024])
                ctx.mark("dve", inst, f"B/evu/{k}")
            if eng == "tensor":
                ctx.wait(E, f"B/evg/{k}")
                if k > 0:
                    ctx.wait(E, f"B/sg/{k-1}/1")
                inst = None
                for half in range(2):
                    for dc in range(16):
                        inst = E.matmul(P[0].ap()[:, half * 512:(half + 1) * 512],
                                        b_wqTg.ap()[:, dc * 128:(dc + 1) * 128],
                                        xn.ap()[:, dc * 1024 + half * 512: dc * 1024 + (half + 1) * 512],
                                        start=(dc == 0), stop=(dc == 15))
                ctx.mark("pe", inst, f"B/mmg/{k}")
                ctx.wait(E, f"B/evu/{k}")
                if k > 0:
                    ctx.wait(E, f"B/hm/{k-1}/1")
                for half in range(2):
                    for dc in range(16):
                        inst = E.matmul(P[1].ap()[:, half * 512:(half + 1) * 512],
                                        b_wqTu.ap()[:, dc * 128:(dc + 1) * 128],
                                        xn.ap()[:, dc * 1024 + half * 512: dc * 1024 + (half + 1) * 512],
                                        start=(dc == 0), stop=(dc == 15))
                ctx.mark("pe", inst, f"B/mmu/{k}")
            if eng == "scalar":
                for half in range(2):
                    ctx.wait(E, f"B/mmg/{k}")
                    if half == 0:
                        if k > 0:
                            ctx.wait(E, f"B/hm/{k-1}/1")
                    else:
                        ctx.wait(E, f"B/hm/{k}/0")
                    inst = E.activation(b_sg.ap(), P[0].ap()[:, half * 512:(half + 1) * 512],
                                        AF.Silu)
                    ctx.mark("act", inst, f"B/sg/{k}/{half}")
            if eng == "vector":
                for half in range(2):
                    ctx.wait(E, f"B/sg/{k}/{half}")
                    ctx.wait(E, f"B/mmu/{k}")
                    inst = E.tensor_tensor(
                        b_h.ap()[:, k * 1024 + half * 512: k * 1024 + (half + 1) * 512],
                        P[1].ap()[:, half * 512:(half + 1) * 512], b_sg.ap(), ALU.mult)
                    ctx.mark("dve", inst, f"B/hm/{k}/{half}")
                if k > 0:
                    ctx.wait(E, f"B/rmm/{k-1}")
                inst = E.tensor_tensor(b_h2.ap(), b_h.ap()[:, k * 1024:(k + 1) * 1024],
                                       b_h.ap()[:, k * 1024:(k + 1) * 1024], ALU.mult)
                ctx.mark("dve", inst, f"B/h2/{k}")
            if eng == "tensor":
                ctx.wait(E, f"B/h2/{k}")
                inst = None
                for half in range(2):
                    inst = E.matmul(P[2].ap()[:, half * 512:(half + 1) * 512], onb.ap(),
                                    b_h2.ap()[:, half * 512:(half + 1) * 512],
                                    start=(k == 0), stop=(k == 63))
                ctx.mark("pe", inst, f"B/rmm/{k}")
        if eng == "vector":
            ctx.wait(E, "B/rmm/63")
            inst = E.tensor_scalar(rstd.ap(), P[2].ap(), 1.0 / HFF, RMS_EPS, ALU.mult, ALU.add)
            ctx.mark("dve", inst, "B/rv0")
        if eng == "scalar":
            ctx.wait(E, "B/rv0")
            inst = E.activation(rstd.ap(), rstd.ap(), AF.Sqrt)
            ctx.mark("act", inst, "B/rv1")
        if eng == "vector":
            ctx.wait(E, "B/rv1")
            inst = E.reciprocal(rstd.ap(), rstd.ap())
            ctx.mark("dve", inst, "B/rstd")
        barrier(ctx, eng, E, "B")

    def phase_D1(ctx, eng, E):
        for db in range(16):
            for hc4 in range(4):
                sl = db * 4 + hc4
                wb = sl % 2
                if eng == "gpsimd":
                    if sl >= 2:
                        ctx.wait(E, f"D1/tp/{sl-2}")
                    ctx.dma(E, "qp", d_wps[wb].ap(),
                            dn["wp"][128 * db:128 * (db + 1), 2048 * hc4:2048 * (hc4 + 1)],
                            f"D1/l/{sl}")
                if eng == "vector":
                    ctx.wait(E, f"D1/l/{sl}")
                if eng in ("vector", "gpsimd", "scalar"):
                    quant_tile(ctx, eng, E, f"D1/q/{sl}", d_wps[wb].ap(), D,
                               d_sc, d_scb, d_rs, d_t, d_t2, d_wh)
                if eng == "tensor":
                    ctx.wait(E, f"D1/q/{sl}/done")
                    if sl > 0:
                        ctx.wait(E, f"D1/ev/{sl-1}")
                    inst = tp16(E, d_wps[wb].ap(), idb.ap())
                    ctx.mark("pe", inst, f"D1/tp/{sl}")
                if eng == "vector":
                    ctx.wait(E, f"D1/tp/{sl}")
                    if sl > 0:
                        ctx.wait(E, f"D1/mm/{sl-1}")
                    p3 = P3b()
                    inst = None
                    for h in range(2):
                        inst = E.tensor_copy(d_wpT.ap()[:, h * 1024:(h + 1) * 1024],
                                             p3[:, h * 1024:(h + 1) * 1024])
                    ctx.mark("dve", inst, f"D1/ev/{sl}")
                if eng == "tensor":
                    ctx.wait(E, f"D1/ev/{sl}")
                    if hc4 == 0 and db > 0:
                        ctx.wait(E, f"D1/sb/{db-1}")
                    inst = None
                    for half in range(2):
                        for hb in range(16):
                            inst = E.matmul(
                                P[0].ap()[:, half * 512:(half + 1) * 512],
                                d_wpT.ap()[:, hb * 128:(hb + 1) * 128],
                                b_h.ap()[:, (hc4 * 16 + hb) * 1024 + half * 512:
                                         (hc4 * 16 + hb) * 1024 + (half + 1) * 512],
                                start=(hc4 == 0 and hb == 0), stop=(hc4 == 3 and hb == 15))
                    ctx.mark("pe", inst, f"D1/mm/{sl}")
            if eng == "vector":
                ctx.wait(E, f"D1/mm/{db*4+3}")
                if db > 0:
                    ctx.wait(E, f"D1/fpark/{db-1}")
                inst = E.tensor_tensor(d_sbf.ap(), P[0].ap(), rstd.ap(), ALU.mult)
                ctx.mark("dve", inst, f"D1/sb/{db}")
            if eng == "sync":
                ctx.wait(E, f"D1/sb/{db}")
                ctx.dma(E, "qs", ffn_park[128 * db:128 * (db + 1), :], d_sbf.ap(),
                        f"D1/fpark/{db}")
        barrier(ctx, eng, E, "D1")

    def phase_D2(ctx, eng, E):
        if eng == "sync":
            ctx.dma(E, "qs", idf.ap(), dn["ident_f"], "D2/idf")
        for dc in range(16):
            if eng == "sync":
                if dc > 0:
                    ctx.wait(E, f"D2/nrm/{dc-1}")
                for hh in (2 * dc, 2 * dc + 1):
                    jj, rr = divmod(hh, 8)
                    ghx = 8 * jj + 2 * (rr % 4) + (rr // 4)
                    ctx.wait(E, f"C/apark/{ghx}")
                    ctx.wait(E, f"C/dpark/{ghx}")
                ctx.dma(E, "qs", e_ab.ap(), attn_park[128 * dc:128 * (dc + 1), :],
                        f"D2/lab/{dc}")
                ctx.dma(E, "qs", e_dn.ap(), den_park[128 * dc:128 * (dc + 1), :],
                        f"D2/la/{dc}")
            if eng == "vector":
                ctx.wait(E, f"D2/lab/{dc}")
                ctx.wait(E, f"D2/la/{dc}")
                E.reciprocal(e_rn.ap(), e_dn.ap())
                inst = E.tensor_tensor(e_attn.ap()[:, dc * 1024:(dc + 1) * 1024],
                                       e_ab.ap(), e_rn.ap(), ALU.mult)
                ctx.mark("dve", inst, f"D2/nrm/{dc}")
        for db in range(16):
            wb = db % 2
            if eng == "gpsimd":
                if db >= 2:
                    ctx.wait(E, f"D2/tp/{db-2}")
                ctx.dma(E, "qp", e_wos[wb].ap(), dn["wo"][128 * db:128 * (db + 1), :],
                        f"D2/lw/{db}")
            if eng == "tensor":
                ctx.wait(E, f"D2/lw/{db}")
                if db > 0:
                    ctx.wait(E, f"D2/ev/{db-1}")
                inst = tp16(E, e_wos[wb].ap(), idb.ap())
                ctx.mark("pe", inst, f"D2/tp/{db}")
            if eng == "vector":
                ctx.wait(E, f"D2/tp/{db}")
                if db > 0:
                    ctx.wait(E, f"D2/mm/{db-1}")
                p3 = P3b()
                inst = None
                for h in range(2):
                    inst = E.tensor_copy(e_woT.ap()[:, h * 1024:(h + 1) * 1024],
                                         p3[:, h * 1024:(h + 1) * 1024])
                ctx.mark("dve", inst, f"D2/ev/{db}")
            if eng == "tensor":
                ctx.wait(E, f"D2/ev/{db}")
                if db == 0:
                    ctx.wait(E, "D2/nrm/15")
                else:
                    ctx.wait(E, f"D2/so/{db-1}")
                inst = None
                for half in range(2):
                    for dc in range(16):
                        inst = E.matmul(P[0].ap()[:, half * 512:(half + 1) * 512],
                                        e_woT.ap()[:, dc * 128:(dc + 1) * 128],
                                        e_attn.ap()[:, dc * 1024 + half * 512: dc * 1024 + (half + 1) * 512],
                                        start=(dc == 0), stop=(dc == 15))
                ctx.mark("pe", inst, f"D2/mm/{db}")
            if eng == "sync":
                fb = db % 2
                if db >= 2:
                    ctx.wait(E, f"D2/ss/{db-2}")
                ctx.wait(E, f"D1/fpark/{db}")
                ctx.dma(E, "qs", e_ft[fb].ap(), ffn_park[128 * db:128 * (db + 1), :],
                        f"D2/lf/{db}")
            if eng == "vector":
                ctx.wait(E, f"D2/mm/{db}")
                inst = E.tensor_scalar(e_so.ap(), P[0].ap(), boc.ap()[:, db:db + 1], None, ALU.add)
                ctx.mark("dve", inst, f"D2/so/{db}")
                ctx.wait(E, f"D2/lf/{db}")
                if db > 0:
                    ctx.wait(E, f"D2/tst/{db-1}")
                inst = E.tensor_tensor(e_ssum.ap(), e_so.ap(), e_ft[db % 2].ap(), ALU.add)
                ctx.mark("dve", inst, f"D2/ss/{db}")
            if eng == "tensor":
                ctx.wait(E, f"D2/ss/{db}")
                if db == 0:
                    ctx.wait(E, "D2/idf")
                else:
                    ctx.wait(E, f"D2/tme/{db-1}")
                inst = None
                for q in range(8):
                    inst = E.transpose(P[1].ap()[:, q * 128:(q + 1) * 128],
                                       e_ssum.ap()[:, q * 128:(q + 1) * 128], idf.ap())
                ctx.mark("pe", inst, f"D2/tst/{db}")
            if eng == "vector":
                ctx.wait(E, f"D2/tst/{db}")
                if db > 0:
                    ctx.wait(E, f"D2/spark/{db-1}")
                inst = E.tensor_copy(e_tm.ap(), P[1].ap())
                ctx.mark("dve", inst, f"D2/tme/{db}")
            if eng == "sync":
                ctx.wait(E, f"D2/tme/{db}")
                dst = sum_park[:, 128 * db:128 * (db + 1)].rearrange("(q p) d -> p q d", p=128)
                ctx.dma(E, "qs", dst, e_tm.ap().rearrange("p (q d) -> p q d", d=128),
                        f"D2/spark/{db}")
        barrier(ctx, eng, E, "D2")

    def phase_D3(ctx, eng, E):
        if debug >= 1 and eng == "sync":
            for nm, src in [("t_kpark", k_park), ("t_vpark", v_park), ("t_apark", attn_park),
                            ("t_dpark", den_park), ("t_fpark", ffn_park), ("t_spark", sum_park)]:
                ctx.dma(E, "qs", dbg_taps[nm], src, f"DBG/{nm}")
            for nm, src in [("t_xn", xn), ("t_h", b_h), ("t_rstd", rstd)]:
                ctx.dma(E, "qs", dbg_taps[nm], src.ap(), f"DBG/{nm}")
            if debug < 2:
                ctx.alias("DBG/wq40", "DBG/t_rstd")
                ctx.alias("DBG/wqT40", "DBG/t_rstd")
        for q in range(8):
            b = q % 2
            if eng == "sync":
                if q >= 2:
                    ctx.wait(E, f"D3/add/{q-2}")
                if q == 0:
                    for db2 in range(16):
                        ctx.wait(E, f"D2/spark/{db2}")
                ctx.dma(E, "qs", f_s[b].ap(), sum_park[128 * q:128 * (q + 1), :], f"D3/ls/{q}")
                ctx.dma(E, "qs", f_x[b].ap(), dn["xo"][128 * q:128 * (q + 1), :], f"D3/l/{q}")
            if eng == "vector":
                ctx.wait(E, f"D3/ls/{q}")
                ctx.wait(E, f"D3/l/{q}")
                if q >= 2:
                    ctx.wait(E, f"D3/st/{q-2}")
                inst = E.tensor_tensor(f_o[b].ap(), f_s[b].ap(), f_x[b].ap(), ALU.add)
                ctx.mark("dve", inst, f"D3/add/{q}")
            if eng == "sync":
                ctx.wait(E, f"D3/add/{q}")
                ctx.dma(E, "qs", y[128 * q:(q + 1) * 128, :], f_o[b].ap(), f"D3/st/{q}")
        if eng == "sync":
            if debug >= 1:
                ctx.wait(E, "DBG/t_rstd")
            ctx.wait(E, "D3/st/7")

    def run_phases(ctx, eng, E):
        phase_A(ctx, eng, E)
        phase_C(ctx, eng, E)
        phase_B(ctx, eng, E)
        phase_D1(ctx, eng, E)
        phase_D2(ctx, eng, E)
        phase_D3(ctx, eng, E)

    ctx0 = Ctx(True, None, registry)
    for eng in ENGS:
        run_phases(ctx0, eng, _MockEng())

    from contextlib import ExitStack
    with ExitStack() as st:
        sems = {}
        for s in SEM_NAMES:
            sems[s] = st.enter_context(nc.semaphore(s))
        nc.all_engine_barrier()
        ksr = bass.get_kernel_semaphore_range()
        clr = range(ksr.start + 5, ksr.stop)
        nc.gpsimd.dma_reset(clr)
        nc.gpsimd.sem_clear(clr)
        nc.all_engine_barrier()
        blk = st.enter_context(nc.Block())
        ctx1 = Ctx(False, sems, registry)
        blk.sync(lambda E: run_phases(ctx1, "sync", E))
        blk.tensor(lambda E: run_phases(ctx1, "tensor", E))
        blk.vector(lambda E: run_phases(ctx1, "vector", E))
        blk.scalar(lambda E: run_phases(ctx1, "scalar", E))
        blk.gpsimd(lambda E: run_phases(ctx1, "gpsimd", E))
        for s in ctx0.counts:
            assert ctx0.counts[s] == ctx1.counts[s], \
                f"pass mismatch {s}: {ctx0.counts[s]} vs {ctx1.counts[s]}"
    return nc


def kernel(L0_attn_x, L0_attn_Wq, L0_attn_bq, L0_attn_Wk, L0_attn_bk, L0_attn_Wv,
           L0_attn_bv, L0_attn_Wo, L0_attn_bo, L0_ffn_w_gate_up, L0_ffn_w_proj):
    x = np.ascontiguousarray(np.asarray(L0_attn_x, np.float32).reshape(T, D))
    wq = np.ascontiguousarray(np.asarray(L0_attn_Wq, np.float32))
    wk = np.ascontiguousarray(np.asarray(L0_attn_Wk, np.float32))
    wv = np.ascontiguousarray(np.asarray(L0_attn_Wv, np.float32))
    wo = np.ascontiguousarray(np.asarray(L0_attn_Wo, np.float32))
    wgu = np.ascontiguousarray(np.asarray(L0_ffn_w_gate_up, np.float32))
    wp = np.ascontiguousarray(np.asarray(L0_ffn_w_proj, np.float32))
    bq = np.asarray(L0_attn_bq, np.float32)
    bk = np.asarray(L0_attn_bk, np.float32)
    bv = np.asarray(L0_attn_bv, np.float32)
    bo = np.asarray(L0_attn_bo, np.float32)

    ident_bf, ident_f, prope_bf, ones_bf, rope_cos, rope_sin = _host_consts()

    bqs = np.zeros((128, 32), np.float32)
    for j in range(4):
        for i in range(4):
            bqs[0:64, 4 * j + i] = bq[64 * (8 * j + i): 64 * (8 * j + i) + 64]
            bqs[64:128, 4 * j + i] = bq[64 * (8 * j + 4 + i): 64 * (8 * j + 4 + i) + 64]
    bkc = np.ascontiguousarray(bk.reshape(4, 128).T)
    bvc = np.ascontiguousarray(bv.reshape(4, 128).T)
    boc = np.ascontiguousarray(bo.reshape(16, 128).T)

    import os
    nc = build_program(debug=int(os.environ.get("KDEBUG", "0")))
    in_maps = []
    for c in range(NCORES):
        idx = np.r_[c * TC:(c + 1) * TC, 0:c * TC, (c + 1) * TC:T]
        in_maps.append({
            "x": np.ascontiguousarray(x[idx]),
            "xo": np.ascontiguousarray(x[c * TC:(c + 1) * TC]),
            "wq": wq, "wk": wk, "wv": wv, "wo": wo, "wgu": wgu, "wp": wp,
            "bqs": bqs, "bkc": bkc, "bvc": bvc, "boc": boc,
            "ident_bf": ident_bf, "ident_f": ident_f, "prope": prope_bf, "ones": ones_bf,
            "rope_cos_a": np.ascontiguousarray(rope_cos[:, idx]),
            "rope_sin_a": np.ascontiguousarray(rope_sin[:, idx]),
            "rope_cos_o": np.ascontiguousarray(rope_cos[:, c * TC:(c + 1) * TC]),
            "rope_sin_o": np.ascontiguousarray(rope_sin[:, c * TC:(c + 1) * TC]),
        })
    res = run_bass_kernel_spmd(nc, in_maps, core_ids=list(range(NCORES)))
    import os
    if int(os.environ.get("KDEBUG", "0")):
        kernel._taps = res.results
    out = np.concatenate([r["y"] for r in res.results], axis=0)
    return out.reshape(1, T, D)
